# version 09
# baseline (speedup 1.0000x reference)
from contextlib import ExitStack

import numpy as np

import concourse.bass as bass
import concourse.mybir as mybir
from concourse.bass_utils import run_bass_kernel_spmd

F32 = mybir.dt.float32
BF16 = mybir.dt.bfloat16
AF = mybir.ActivationFunctionType
ALU = mybir.AluOpType
AX = mybir.AxisListType

R = 8
D = 2048
S = 8192
TL = 1024
HK = D // 128
IK = 44
INTER = 5632
NHL = 2
SCALE = float(1.0 / np.sqrt(128.0))
EPS = 1e-5
NEG = -1.0e30
INV_D = float(1.0 / D)


class Ctr:

    def __init__(self, sem):
        self.sem = sem
        self.n = 0

    def tok(self):
        return (self, self.n)


class Prog:

    ENGINES = ("sync", "gpsimd", "tensor", "vector", "scalar")

    def __init__(self):
        self.items = {e: [] for e in self.ENGINES}

    def emit(self, eng, fn, waits=(), incs=()):
        rec = []
        for c, k in incs:
            c.n += k
            rec.append((c, k))
        self.items[eng].append((fn, [w for w in waits if w is not None], rec))

    def inc_last(self, eng, ctr, k=1):
        ctr.n += k
        self.items[eng][-1][2].append((ctr, k))

    def finalize(self, block):
        def run(eng_obj, eng_name):
            last_wait = {}
            for fn, waits, incs in self.items[eng_name]:
                for ctr, val in waits:
                    if val <= 0:
                        continue
                    if last_wait.get(id(ctr), -1) >= val:
                        continue
                    eng_obj.wait_ge(ctr.sem, val)
                    last_wait[id(ctr)] = val
                if fn is None:
                    continue
                instr = fn(eng_obj)
                for ctr, k in incs:
                    instr.then_inc(ctr.sem, k)

        if self.items["sync"]:
            block.sync(lambda e: run(e, "sync"))
        if self.items["gpsimd"]:
            block.gpsimd(lambda e: run(e, "gpsimd"))
        if self.items["tensor"]:
            block.tensor(lambda e: run(e, "tensor"))
        if self.items["vector"]:
            block.vector(lambda e: run(e, "vector"))
        if self.items["scalar"]:
            block.scalar(lambda e: run(e, "scalar"))


def build(dbg=False, phases=6, probe=None):
    nc = bass.Bass("TRN2", target_bir_lowering=False,
                   detect_race_conditions=False)
    KD = "ExternalOutput" if dbg else "Internal"

    xT = nc.dram_tensor("xT", [128, HK, TL], F32, kind="ExternalInput").ap()
    wq = nc.dram_tensor("wq", [128, HK, 256], F32, kind="ExternalInput").ap()
    wk = nc.dram_tensor("wk", [128, HK, 256], F32, kind="ExternalInput").ap()
    wv = nc.dram_tensor("wv", [128, HK, 256], F32, kind="ExternalInput").ap()
    wo = nc.dram_tensor("wo", [128, 2, D], F32, kind="ExternalInput").ap()
    gw = nc.dram_tensor("gw", [128, IK, HK, 128], F32, kind="ExternalInput").ap()
    uw = nc.dram_tensor("uw", [128, IK, HK, 128], F32, kind="ExternalInput").ap()
    dw = nc.dram_tensor("dw", [128, IK, 4, 512], F32, kind="ExternalInput").ap()
    gs = nc.dram_tensor("gs", [128, IK], F32, kind="ExternalInput").ap()
    us = nc.dram_tensor("us", [128, IK], F32, kind="ExternalInput").ap()
    ds = nc.dram_tensor("ds", [128, HK], F32, kind="ExternalInput").ap()

    yT = nc.dram_tensor("yT", [128, HK, TL], F32, kind="ExternalOutput").ap()

    x_dram = nc.dram_tensor("x_dram", [128, HK, TL], F32, kind=KD).ap()
    cc_xn_in = [nc.dram_tensor(f"cc_xn_in{g}", [128, HK * 512], F32,
                               kind=KD).ap() for g in range(2)]
    cc_xn_out = [nc.dram_tensor(f"cc_xn_out{g}", [R * 128, HK * 512],
                                F32).ap() for g in range(2)]
    qd = nc.dram_tensor("qd", [NHL, 128, S], F32, kind=KD).ap()
    kd = nc.dram_tensor("kd", [NHL, 128, S], F32, kind=KD).ap()
    vtm = nc.dram_tensor("vtm", [S, 256], F32, kind=KD).ap()
    attn_part = nc.dram_tensor("attn_part", [S, D], F32, kind=KD).ap()
    attn_own = nc.dram_tensor("attn_own", [TL, D], F32, kind=KD).ap()

    p = Prog()

    with ExitStack() as stk:
        e = stk.enter_context

        ones_col = e(nc.sbuf_tensor("ones_col", [1, 128], F32))
        ones128 = e(nc.sbuf_tensor("ones128", [128, 1], F32))
        ones_sq = e(nc.sbuf_tensor("ones_sq", [128, 128], F32))
        ident = e(nc.sbuf_tensor("ident", [128, 128], F32))
        masks = e(nc.sbuf_tensor("masks", [128, 4 * 512], F32))
        gs_sb = e(nc.sbuf_tensor("gs_sb", [128, IK], F32))
        us_sb = e(nc.sbuf_tensor("us_sb", [128, IK], F32))
        dsh_sb = e(nc.sbuf_tensor("dsh_sb", [128, HK], F32))
        maxQn = e(nc.sbuf_tensor("maxQn", [1, NHL * 16], F32))
        maxKn = e(nc.sbuf_tensor("maxKn", [1, NHL], F32))
        negCs = e(nc.sbuf_tensor("negCs", [1, NHL * 16], F32))
        rows = e(nc.sbuf_tensor("rows", [1, 6 * 512], F32))

        xb_sb = e(nc.sbuf_tensor("xb_sb", [128, 2 * 512], F32))
        sq_sb = e(nc.sbuf_tensor("sq_sb", [128, 512], F32))
        tmp_sb = e(nc.sbuf_tensor("tmp_sb", [128, 512], F32))
        out_sb = e(nc.sbuf_tensor("out_sb", [128, 2 * 512], F32))
        xr_sb = e(nc.sbuf_tensor("xr_sb", [128, 2 * 512], F32))
        negc_sb = e(nc.sbuf_tensor("negc_sb", [128, 1], F32))

        arena = e(nc.sbuf_tensor("arena", [128, 43008], F32))
        A = arena
        xnf_sb = A[:, 0:512]
        r1f_sb = A[:, 512:1024]
        sg_sb = A[:, 1536:2048]
        hdn_sb = A[:, 2048:24576]
        xnp_sb = A[:, 24576:36864].bitcast(BF16)
        gbf_sb = A[:, 36864:40960].bitcast(BF16)
        dbf_sb = A[:, 40960:41472].bitcast(BF16)
        hsp_sb = A[:, 41472:43008].bitcast(BF16)
        xnt_sb = A[:, 0:16384]
        wq_sb = A[:, 16384:20480]
        wk_sb = A[:, 20480:24576]
        wv_sb = A[:, 24576:28672]
        ev_sb = A[:, 28672:32768]
        q_sb = A[:, 0:1024]
        k_sb = A[:, 1024:3072]
        v_sb = A[:, 3072:5120]
        p_sb = A[:, 5120:7168]
        linv_sb = A[:, 7168:7680]
        oT_sb = A[:, 7680:8704]
        wo_sb = A[:, 8704:12800]

        B = [e(nc.psum_tensor(f"pb{i}", [128, 512], F32)) for i in range(8)]

        def C(name):
            return Ctr(e(nc.semaphore(name)))

        MM, VE, ACT, GP, CC = C("s_mm"), C("s_ve"), C("s_ac"), C("s_gp"), \
            C("s_cc")
        sc_ld = C("sc_ld")
        wqkv_ld = C("wqkv_ld")
        wo_ld = C("wo_ld")
        xb_ld = [C("xb_ld0"), C("xb_ld1")]
        wgu_ld = [C("wgu_ld0"), C("wgu_ld1")]
        wd_ld = [C("wd_ld0"), C("wd_ld1")]
        xr_ld = C("xr_ld")
        xnt_ld = [C("xnt_ld0"), C("xnt_ld1")]
        k_ld = [C(f"k_ld{i}") for i in range(4)]
        v_ld = [C(f"v_ld{i}") for i in range(4)]
        q_ld = C("q_ld")
        out_st = [C("out_st0"), C("out_st1"), C("out_st2"), C("out_st3")]
        ev_st = [C("ev_st0"), C("ev_st1"), C("ev_st2"), C("ev_st3")]
        vt_st = [C("vt_st0"), C("vt_st1")]

        block = e(nc.Block())

        def dma(dst, src):
            return lambda g: g.dma_start(out=dst, in_=src)

        def all_store_toks():
            return [s.tok() for s in out_st + ev_st + vt_st]

        p.emit("sync", dma(gs_sb[:], gs), incs=[(sc_ld, 16)])
        p.emit("sync", dma(us_sb[:], us), incs=[(sc_ld, 16)])
        p.emit("sync", dma(dsh_sb[:], ds), incs=[(sc_ld, 16)])
        t_scales = sc_ld.tok()

        p.emit("vector", lambda v: v.memset(ones_col[:], 1.0), incs=[(VE, 1)])
        p.emit("vector", lambda v: v.memset(ones128[:], 1.0), incs=[(VE, 1)])
        p.emit("vector", lambda v: v.memset(ones_sq[:], 1.0), incs=[(VE, 1)])
        p.emit("vector", lambda v: v.memset(maxKn[:], 0.0), incs=[(VE, 1)])
        p.emit("vector", lambda v: v.tensor_scalar(
            dsh_sb[:], dsh_sb[:], 0.5, None, ALU.mult),
            waits=[t_scales], incs=[(VE, 1)])
        ve_const = VE.n

        p.emit("gpsimd", lambda g: g.memset(ident[:], 0.0), incs=[(GP, 1)])
        p.emit("gpsimd", lambda g: g.affine_select(
            out=ident[:], in_=ident[:], compare_op=ALU.not_equal, fill=1.0,
            base=0, pattern=[[-1, 128]], channel_multiplier=1),
            waits=[(GP, GP.n)], incs=[(GP, 1)])
        for sft in range(4):
            p.emit("gpsimd", lambda g, sft=sft: g.memset(
                masks[:, sft * 512:(sft + 1) * 512], 0.0), incs=[(GP, 1)])
            p.emit("gpsimd", lambda g, sft=sft: g.affine_select(
                out=masks[:, sft * 512:(sft + 1) * 512],
                in_=masks[:, sft * 512:(sft + 1) * 512],
                compare_op=ALU.is_ge, fill=NEG,
                base=-128 * sft, pattern=[[1, 512]], channel_multiplier=-1),
                waits=[(GP, GP.n)], incs=[(GP, 1)])
        gp_const = GP.n

        bank_free = {}
        xfree = [None, None]
        sqfree = [None]
        outfree = [None] * 2
        xrfree = [None] * 2

        def ln_stats(src, c0, c1, extra_first_wait=None):
            for hc in range(HK):
                sl = hc % 2
                p.emit("sync", dma(xb_sb[:, sl * 512:(sl + 1) * 512],
                                   src[:, hc, c0:c1]),
                       waits=[xfree[sl]] +
                             ([extra_first_wait] if hc == 0 else []),
                       incs=[(xb_ld[sl], 16)])
                t_x = xb_ld[sl].tok()
                p.emit("tensor", lambda t, sl=sl, hc=hc: t.matmul(
                    B[0][:], ones_sq[:],
                    xb_sb[:, sl * 512:(sl + 1) * 512],
                    start=(hc == 0), stop=(hc == HK - 1)),
                    waits=[t_x, (VE, ve_const)] +
                          ([bank_free.get(0)] if hc == 0 else []))
                p.emit("vector", lambda v, sl=sl: v.tensor_mul(
                    sq_sb[:],
                    xb_sb[:, sl * 512:(sl + 1) * 512],
                    xb_sb[:, sl * 512:(sl + 1) * 512]),
                    waits=[t_x, sqfree[0]], incs=[(VE, 1)])
                ve_sq = VE.n
                p.emit("tensor", lambda t, hc=hc: t.matmul(
                    B[1][:], ones_sq[:],
                    sq_sb[:],
                    start=(hc == 0), stop=(hc == HK - 1)),
                    waits=[(VE, ve_sq)] +
                          ([bank_free.get(1)] if hc == 0 else []),
                    incs=[(MM, 1)])
                xfree[sl] = (MM, MM.n)
                sqfree[0] = (MM, MM.n)
            return MM.n

        def ln_rows(mm_stats):
            mrow = rows[:, 0:512]
            srow = rows[:, 512:1024]
            m2row = rows[:, 1024:1536]
            vrow = rows[:, 1536:2048]
            rsrow = rows[:, 2048:2560]
            mrsrow = rows[:, 2560:3072]
            p.emit("vector", lambda v: v.tensor_scalar(
                mrow, B[0][0:1, :], INV_D, None, ALU.mult),
                waits=[(MM, mm_stats)])
            p.emit("vector", lambda v: v.tensor_scalar(
                srow, B[1][0:1, :], INV_D, None, ALU.mult))
            p.emit("vector", lambda v: v.tensor_mul(m2row, mrow, mrow))
            p.emit("vector", lambda v: v.tensor_sub(vrow, srow, m2row))
            p.emit("vector", lambda v: v.tensor_single_scalar(
                vrow, vrow, EPS, ALU.add), incs=[(VE, 1)])
            p.emit("scalar", lambda s: s.activation(
                srow, vrow, AF.Sqrt), waits=[(VE, VE.n)], incs=[(ACT, 1)])
            p.emit("vector", lambda v: v.reciprocal(rsrow, srow),
                   waits=[(ACT, ACT.n)])
            p.emit("vector", lambda v: v.tensor_mul(mrsrow, mrow, rsrow),
                   incs=[(VE, 1)])
            ve_rows = VE.n
            p.emit("tensor", lambda t: t.matmul(
                B[2][:], ones_col[:], rsrow, start=True, stop=True),
                waits=[(VE, ve_rows), bank_free.get(2)])
            p.emit("tensor", lambda t: t.matmul(
                B[3][:], ones_col[:], mrsrow, start=True, stop=True),
                waits=[bank_free.get(3)], incs=[(MM, 1)])
            return MM.n

        def ffn_pass(src, dst, first_wait=None):
            hdn_ready = [None] * IK
            wfree = [None, None]
            dfree = [None, None]
            gb_free = [None, None]
            sg_free = [None]
            obfree = [None] * 4
            hspfree = [None, None]
            xnp_free = [None]
            xh = xnp_sb[:, 0:8192]
            xm = xnp_sb[:, 8192:16384]
            xl = xnp_sb[:, 16384:24576]
            for q5 in range(2):
                c0, c1 = q5 * 512, (q5 + 1) * 512
                mm_stats = ln_stats(src, c0, c1,
                                    first_wait if q5 == 0 else None)
                mm_bcast = ln_rows(mm_stats)
                for hc in range(HK):
                    sl = hc % 2
                    s5 = slice(hc * 512, (hc + 1) * 512)
                    p.emit("sync", dma(xb_sb[:, sl * 512:(sl + 1) * 512],
                                       src[:, hc, c0:c1]),
                           waits=[xfree[sl]], incs=[(xb_ld[sl], 16)])
                    t_x = xb_ld[sl].tok()
                    p.emit("vector", lambda v, sl=sl: v.tensor_mul(
                        tmp_sb[:], xb_sb[:, sl * 512:(sl + 1) * 512], B[2][:]),
                        waits=[t_x, (MM, mm_bcast)] +
                              ([xnp_free[0]] if hc == 0 else []))
                    p.emit("vector", lambda v: v.tensor_sub(
                        xnf_sb[:], tmp_sb[:], B[3][:]))
                    p.emit("vector", lambda v, s5=s5: v.tensor_copy(
                        xh[:, s5], xnf_sb[:]))
                    p.emit("vector", lambda v, s5=s5: v.tensor_sub(
                        xm[:, s5], xnf_sb[:], xh[:, s5]))
                    p.emit("vector", lambda v, s5=s5: v.tensor_sub(
                        r1f_sb[:], xnf_sb[:], xh[:, s5]))
                    p.emit("vector", lambda v, s5=s5: v.tensor_sub(
                        xl[:, s5], r1f_sb[:], xm[:, s5]), incs=[(VE, 1)])
                    xfree[sl] = (VE, VE.n)
                ve_xn = VE.n
                bank_free[2] = (VE, ve_xn)
                bank_free[3] = (VE, ve_xn)

                for isl in range(IK):
                    sl = isl % 2
                    gslab = gbf_sb[:, (2 * sl) * 2048:(2 * sl + 1) * 2048]
                    uslab = gbf_sb[:, (2 * sl + 1) * 2048:(2 * sl + 2) * 2048]
                    p.emit("gpsimd", dma(
                        gslab.rearrange("p (c n) -> p c n", n=128),
                        gw[:, isl]), waits=[wfree[sl]],
                        incs=[(wgu_ld[sl], 16)])
                    p.emit("gpsimd", dma(
                        uslab.rearrange("p (c n) -> p c n", n=128),
                        uw[:, isl]), incs=[(wgu_ld[sl], 16)])
                    t_w = wgu_ld[sl].tok()
                    gbank, ubank = B[2 * sl], B[2 * sl + 1]
                    for wslab, bank, gfree in ((gslab, gbank, gb_free[sl]),
                                               (uslab, ubank, None)):
                        first = True
                        for c in range(HK):
                            for xp in (xh, xm, xl):
                                p.emit("tensor", lambda t, wslab=wslab, c=c,
                                       bank=bank, xp=xp: t.matmul(
                                           bank[:],
                                           wslab[:, c * 128:(c + 1) * 128],
                                           xp[:, c * 512:(c + 1) * 512],
                                           start=(c == 0 and xp is xh),
                                           stop=(c == HK - 1 and xp is xl)),
                                       waits=([t_w, (VE, ve_xn), gfree]
                                              if first else []))
                                first = False
                        p.inc_last("tensor", MM, 1)
                        if wslab is gslab:
                            mm_gate = MM.n
                        else:
                            mm_up = MM.n
                    wfree[sl] = (MM, MM.n)
                    p.emit("scalar", lambda s, gbank=gbank, isl=isl:
                           s.activation(sg_sb[:], gbank[:], AF.Silu,
                                        scale=gs_sb[:, isl:isl + 1]),
                           waits=[(MM, mm_gate), sg_free[0], t_scales],
                           incs=[(ACT, 1)])
                    ac_silu = ACT.n
                    p.emit("vector", lambda v, ubank=ubank, isl=isl:
                           v.scalar_tensor_tensor(
                               hdn_sb[:, isl * 512:(isl + 1) * 512], ubank[:],
                               us_sb[:, isl:isl + 1], sg_sb[:],
                               ALU.mult, ALU.mult),
                           waits=[(MM, mm_up), (ACT, ac_silu)], incs=[(VE, 1)])
                    sg_free[0] = (VE, VE.n)
                    gb_free[sl] = (VE, VE.n)
                    hdn_ready[isl] = (VE, VE.n)
                xnp_free[0] = (MM, MM.n)

                for hg in range(4):
                    t_xr = [None, None]
                    for hc4 in range(2):
                        p.emit("sync", dma(
                            xr_sb[:, hc4 * 512:(hc4 + 1) * 512],
                            src[:, hg * 4 + hc4, c0:c1]),
                            waits=[xrfree[hc4]], incs=[(xr_ld, 16)])
                        xrfree[hc4] = None
                        t_xr[hc4] = xr_ld.tok()
                    for ic in range(IK):
                        sl = ic % 2
                        dslab = dbf_sb[:, sl * 512:(sl + 1) * 512]
                        p.emit("gpsimd", dma(dslab, dw[:, ic, hg]),
                               waits=[dfree[sl]], incs=[(wd_ld[sl], 16)])
                        t_d = wd_ld[sl].tok()
                        hs = hsp_sb[:, sl * 1536:(sl + 1) * 1536]
                        hh, hm, hl = hs[:, 0:512], hs[:, 512:1024], \
                            hs[:, 1024:1536]
                        hdn_ic = hdn_sb[:, ic * 512:(ic + 1) * 512]
                        p.emit("vector", lambda v, hh=hh, hdn_ic=hdn_ic:
                               v.tensor_copy(hh, hdn_ic),
                               waits=[hdn_ready[ic], hspfree[sl]])
                        p.emit("vector", lambda v, hm=hm, hdn_ic=hdn_ic,
                               hh=hh: v.tensor_sub(hm, hdn_ic, hh))
                        p.emit("vector", lambda v, hdn_ic=hdn_ic, hh=hh:
                               v.tensor_sub(r1f_sb[:], hdn_ic, hh))
                        p.emit("vector", lambda v, hl=hl, hm=hm:
                               v.tensor_sub(hl, r1f_sb[:], hm),
                               incs=[(VE, 1)])
                        ve_hsp = VE.n
                        first = True
                        for hc4 in range(4):
                            for pi_, hp in enumerate((hh, hm, hl)):
                                st_ = (ic == 0 and pi_ == 0)
                                sp_ = (ic == IK - 1 and pi_ == 2)
                                p.emit("tensor", lambda t, dslab=dslab,
                                       hc4=hc4, hp=hp, st_=st_, sp_=sp_:
                                       t.matmul(
                                           B[4 + hc4][:],
                                           dslab[:, hc4 * 128:(hc4 + 1) * 128],
                                           hp, start=st_, stop=sp_),
                                       waits=([t_d, (VE, ve_hsp)] +
                                              ([obfree[0], obfree[1],
                                                obfree[2], obfree[3]]
                                               if ic == 0 else [])
                                              if first else []))
                                first = False
                        p.inc_last("tensor", MM, 1)
                        dfree[sl] = (MM, MM.n)
                        hspfree[sl] = (MM, MM.n)
                    mm_down = MM.n
                    for hc4 in range(4):
                        hcg = hg * 4 + hc4
                        sl2 = hc4 % 2
                        if hc4 >= 2:
                            p.emit("sync", dma(
                                xr_sb[:, sl2 * 512:(sl2 + 1) * 512],
                                src[:, hcg, c0:c1]),
                                waits=[xrfree[sl2]], incs=[(xr_ld, 16)])
                            t_xr[sl2] = xr_ld.tok()
                        p.emit("vector", lambda v, sl2=sl2, hc4=hc4, hcg=hcg:
                               v.scalar_tensor_tensor(
                                   out_sb[:, sl2 * 512:(sl2 + 1) * 512],
                                   B[4 + hc4][:], dsh_sb[:, hcg:hcg + 1],
                                   xr_sb[:, sl2 * 512:(sl2 + 1) * 512],
                                   ALU.mult, ALU.add),
                               waits=[(MM, mm_down), t_xr[sl2],
                                      outfree[sl2]],
                               incs=[(VE, 1)])
                        obfree[hc4] = (VE, VE.n)
                        xrfree[sl2] = (VE, VE.n)
                        p.emit("sync", dma(
                            dst[:, hcg, c0:c1],
                            out_sb[:, sl2 * 512:(sl2 + 1) * 512]),
                            waits=[(VE, VE.n)], incs=[(out_st[sl2], 16)])
                        outfree[sl2] = out_st[sl2].tok()

        ffn_pass(xT, x_dram)
        barrier_p1 = all_store_toks()

        if phases >= 2:
            onfree = [None, None]
            for q5 in range(2):
                c0, c1 = q5 * 512, (q5 + 1) * 512
                mm_stats = ln_stats(x_dram, c0, c1,
                                    barrier_p1[0] if q5 == 0 else None)
                mm_bcast = ln_rows(mm_stats)
                for hc in range(HK):
                    sl = hc % 2
                    p.emit("sync", dma(xb_sb[:, sl * 512:(sl + 1) * 512],
                                       x_dram[:, hc, c0:c1]),
                           waits=[xfree[sl]], incs=[(xb_ld[sl], 16)])
                    t_x = xb_ld[sl].tok()
                    p.emit("vector", lambda v, sl=sl: v.tensor_mul(
                        tmp_sb[:], xb_sb[:, sl * 512:(sl + 1) * 512], B[2][:]),
                        waits=[t_x, (MM, mm_bcast), onfree[sl]])
                    p.emit("vector", lambda v, sl=sl: v.tensor_sub(
                        out_sb[:, sl * 512:(sl + 1) * 512], tmp_sb[:], B[3][:]),
                        incs=[(VE, 1)])
                    xfree[sl] = (VE, VE.n)
                    p.emit("sync", dma(
                        cc_xn_in[q5][:, hc * 512:(hc + 1) * 512],
                        out_sb[:, sl * 512:(sl + 1) * 512]),
                        waits=[(VE, VE.n)], incs=[(out_st[sl], 16)])
                    onfree[sl] = out_st[sl].tok()
                    outfree[sl] = out_st[sl].tok()
                bank_free[2] = (VE, VE.n)
                bank_free[3] = (VE, VE.n)
                p.emit("gpsimd", lambda g, q5=q5: g.collective_compute(
                    "AllGather", ALU.bypass,
                    replica_groups=[list(range(R))],
                    ins=[cc_xn_in[q5][:]], outs=[cc_xn_out[q5][:]]),
                    waits=[out_st[0].tok(), out_st[1].tok()], incs=[(CC, 1)])
            cc_xn_done = CC.n


        if phases >= 3:
            p.emit("sync", dma(wq_sb.rearrange("p (c n) -> p c n", n=256), wq),
                   waits=[(CC, cc_xn_done - 1)], incs=[(wqkv_ld, 16)])
            p.emit("sync", dma(wk_sb.rearrange("p (c n) -> p c n", n=256), wk),
                   incs=[(wqkv_ld, 16)])
            p.emit("sync", dma(wv_sb.rearrange("p (c n) -> p c n", n=256), wv),
                   incs=[(wqkv_ld, 16)])
            t_wqkv = wqkv_ld.tok()
            xn_view = [cco.rearrange("(r p) (c t) -> r p c t", p=128, t=512)
                       for cco in cc_xn_out]
            xnt_free = [None, None]
            ev_free = [None] * 8
            b7_free = None
            for ni, nt in enumerate(list(range(0, 16, 2)) +
                                    list(range(1, 16, 2))):
                blk = nt // 2
                q5 = nt % 2
                sl = ni % 2
                xnt = xnt_sb[:, sl * HK * 512:(sl + 1) * HK * 512]
                p.emit("sync", dma(
                    xnt.rearrange("p (c t) -> p c t", t=512),
                    xn_view[q5][blk, :, :, :]),
                    waits=[xnt_free[sl], (CC, cc_xn_done - 1 + q5)],
                    incs=[(xnt_ld[sl], 16)])
                t_xnt = xnt_ld[sl].tok()
                for pi, wsb in ((0, wq_sb), (1, wk_sb), (2, wv_sb)):
                    for dc in range(2):
                        bank = B[pi * 2 + dc]
                        for c in range(HK):
                            p.emit("tensor", lambda t, wsb=wsb, c=c, dc=dc,
                                   bank=bank, xnt=xnt: t.matmul(
                                       bank[:],
                                       wsb[:, c * 256 + dc * 128:
                                           c * 256 + dc * 128 + 128],
                                       xnt[:, c * 512:(c + 1) * 512],
                                       start=(c == 0), stop=(c == HK - 1)),
                                   waits=([t_xnt, t_wqkv,
                                           bank_free.get(pi * 2 + dc)]
                                          if c == 0 else []))
                        p.inc_last("tensor", MM, 1)
                        bank_free[pi * 2 + dc] = None
                mm_qkv = MM.n
                xnt_free[sl] = (MM, MM.n)
                for pi, dest in ((0, qd), (1, kd)):
                    for dc in range(2):
                        bank = B[pi * 2 + dc]
                        mm_bank = mm_qkv - 5 + (pi * 2 + dc)
                        esl = pi * 2 + dc
                        ev = ev_sb[:, esl * 512:(esl + 1) * 512]
                        p.emit("vector", lambda v, ev=ev, bank=bank: v.tensor_copy(
                            ev, bank[:]),
                            waits=[(MM, mm_bank), ev_free[esl]], incs=[(VE, 1)])
                        bank_free[pi * 2 + dc] = (VE, VE.n)
                        p.emit("sync", dma(dest[dc, :, nt * 512:nt * 512 + 512],
                                           ev),
                               waits=[(VE, VE.n)], incs=[(ev_st[esl], 16)])
                        st_ev = ev_st[esl].tok()
                        p.emit("vector", lambda v, ev=ev: v.tensor_mul(
                            tmp_sb[:], ev, ev), waits=[b7_free], incs=[(VE, 1)])
                        p.emit("tensor", lambda t: t.matmul(
                            B[7][:], ones_sq[:], tmp_sb[:],
                            start=True, stop=True),
                            waits=[(VE, VE.n)], incs=[(MM, 1)])
                        p.emit("vector", lambda v: v.reduce_max(
                            rows[:, 0:1], B[7][0:1, :], axis=AX.X),
                            waits=[(MM, MM.n)], incs=[(VE, 1)])
                        b7_free = (VE, VE.n)
                        t_red = (VE, VE.n)
                        if pi == 0:
                            p.emit("vector", lambda v, dc=dc, nt=nt: v.tensor_copy(
                                maxQn[0:1, dc * 16 + nt:dc * 16 + nt + 1],
                                rows[:, 0:1]), waits=[t_red], incs=[(VE, 1)])
                        else:
                            p.emit("vector", lambda v, dc=dc: v.tensor_tensor(
                                maxKn[0:1, dc:dc + 1], maxKn[0:1, dc:dc + 1],
                                rows[:, 0:1], ALU.max), waits=[t_red],
                                incs=[(VE, 1)])
                        ev_free[esl] = st_ev
                for dc in range(2):
                    bank = B[4 + dc]
                    mm_bank = mm_qkv - 1 + dc
                    esl = 4 + dc
                    ev = ev_sb[:, esl * 512:(esl + 1) * 512]
                    p.emit("vector", lambda v, ev=ev, bank=bank: v.tensor_copy(
                        ev, bank[:]), waits=[(MM, mm_bank), ev_free[esl]],
                        incs=[(VE, 1)])
                    ve_ev = VE.n
                    bank_free[4 + dc] = (VE, VE.n)
                    for tt in range(4):
                        p.emit("tensor", lambda t, ev=ev, tt=tt: t.transpose(
                            B[6][:, tt * 128:(tt + 1) * 128],
                            ev[:, tt * 128:(tt + 1) * 128], ident[:]),
                            waits=([(VE, ve_ev), (GP, gp_const),
                                    bank_free.get(6)] if tt == 0 else []))
                    p.inc_last("tensor", MM, 1)
                    ev_free[esl] = (MM, MM.n)
                    esl2 = 6 + dc
                    ev2 = ev_sb[:, esl2 * 512:(esl2 + 1) * 512]
                    p.emit("vector", lambda v, ev2=ev2: v.tensor_copy(
                        ev2, B[6][:]), waits=[(MM, MM.n), ev_free[esl2]],
                        incs=[(VE, 1)])
                    bank_free[6] = (VE, VE.n)
                    p.emit("sync", dma(
                        vtm[nt * 512:(nt + 1) * 512,
                            dc * 128:(dc + 1) * 128].rearrange(
                                "(tt p) d -> p tt d", p=128),
                        ev2.rearrange("p (tt d) -> p tt d", d=128)),
                        waits=[(VE, VE.n)], incs=[(vt_st[dc], 16)])
                    ev_free[esl2] = vt_st[dc].tok()
            barrier_qkv = [s.tok() for s in ev_st + vt_st]

            for dc in range(NHL):
                p.emit("vector", lambda v, dc=dc: v.tensor_scalar(
                    negCs[0:1, dc * 16:(dc + 1) * 16],
                    maxQn[0:1, dc * 16:(dc + 1) * 16],
                    maxKn[0:1, dc:dc + 1], None, ALU.mult), incs=[(VE, 1)])
            p.emit("scalar", lambda s: s.activation(negCs[:], negCs[:], AF.Sqrt),
                   waits=[(VE, VE.n)], incs=[(ACT, 1)])
            p.emit("vector", lambda v: v.tensor_scalar(
                negCs[:], negCs[:], -SCALE, None, ALU.mult),
                waits=[(ACT, ACT.n)], incs=[(VE, 1)])
            ve_ctab = VE.n


        if phases >= 4:
            p.emit("sync", dma(wo_sb.rearrange("p (c n) -> p c n", n=D), wo),
                   waits=barrier_qkv, incs=[(wo_ld, 16)])
            t_wo = wo_ld.tok()
            kfree = [None] * 4
            vfree = [None] * 4
            pfree = [None] * 4
            qfree = [None, None]
            l_free = None
            o_free = None
            negc_free = None
            linv_free = None
            oT_free = [None, None]
            wo_bfree = [None, None]
            wo_outfree = [outfree[0], outfree[1]]
            sbank_free = [None, None]
            for qc in range(16):
                for h in range(NHL):
                    qsl = (qc * NHL + h) % 2
                    qv = q_sb[:, qsl * 512:(qsl + 1) * 512]
                    p.emit("sync", dma(qv, qd[h, :, qc * 512:qc * 512 + 512]),
                           waits=[qfree[qsl]] +
                                 (barrier_qkv if qc == 0 and h == 0 else []),
                           incs=[(q_ld, 16)])
                    t_q = q_ld.tok()
                    p.emit("tensor", lambda t, h=h, qc=qc: t.matmul(
                        B[4][:, 0:1], ones_col[:],
                        negCs[0:1, h * 16 + qc:h * 16 + qc + 1],
                        start=True, stop=True),
                        waits=[(VE, ve_ctab), negc_free], incs=[(MM, 1)])
                    p.emit("vector", lambda v: v.tensor_copy(
                        negc_sb[:], B[4][:, 0:1]),
                        waits=[(MM, MM.n)], incs=[(VE, 1)])
                    ve_negc = VE.n
                    for j in range(qc + 1):
                        ksl = j % 4
                        vsl = j % 4
                        p.emit("sync", dma(k_sb[:, ksl * 512:(ksl + 1) * 512],
                                           kd[h, :, j * 512:j * 512 + 512]),
                               waits=[kfree[ksl]], incs=[(k_ld[ksl], 16)])
                        t_k = k_ld[ksl].tok()
                        p.emit("sync", dma(
                            v_sb[:, vsl * 512:(vsl + 1) * 512].rearrange(
                                "p (tt d) -> p tt d", d=128),
                            vtm[j * 512:(j + 1) * 512,
                                h * 128:(h + 1) * 128].rearrange(
                                    "(tt p) d -> p tt d", p=128)),
                            waits=[vfree[vsl]], incs=[(v_ld[vsl], 16)])
                        t_v = v_ld[vsl].tok()
                        diag = (j == qc)
                        first = (j == 0)
                        last = (j == qc)
                        for sft in range(4):
                            sb = sft % 2
                            p.emit("tensor", lambda t, sb=sb, ksl=ksl, sft=sft,
                                   qv=qv: t.matmul(
                                       B[sb][:],
                                       k_sb[:, ksl * 512 + sft * 128:
                                            ksl * 512 + sft * 128 + 128],
                                       qv, start=True, stop=True),
                                   waits=[t_k, t_q, sbank_free[sb]],
                                   incs=[(MM, 1)])
                            mm_s = MM.n
                            if diag:
                                p.emit("vector", lambda v, sb=sb, sft=sft:
                                       v.tensor_add(B[sb][:], B[sb][:],
                                                    masks[:, sft * 512:
                                                          (sft + 1) * 512]),
                                       waits=[(MM, mm_s), (GP, gp_const)],
                                       incs=[(VE, 1)])
                                dep = (VE, VE.n)
                            else:
                                dep = (MM, mm_s)
                            p.emit("scalar", lambda s, sb=sb, sft=sft:
                                   s.activation(
                                       p_sb[:, sft * 512:(sft + 1) * 512],
                                       B[sb][:], AF.Exp, bias=negc_sb[:],
                                       scale=SCALE),
                                   waits=[dep, pfree[sft], (VE, ve_negc)],
                                   incs=[(ACT, 1)])
                            ac_exp = ACT.n
                            sbank_free[sb] = (ACT, ACT.n)
                            p.emit("tensor", lambda t, sft=sft, first=first,
                                   last=last: t.matmul(
                                       B[2][:], ones_sq[:],
                                       p_sb[:, sft * 512:(sft + 1) * 512],
                                       start=(first and sft == 0),
                                       stop=(last and sft == 3)),
                                   waits=[(ACT, ac_exp)] +
                                         ([l_free] if first and sft == 0 else []))
                            p.emit("tensor", lambda t, sft=sft, vsl=vsl,
                                   first=first, last=last: t.matmul(
                                       B[3][:], v_sb[:, vsl * 512 + sft * 128:
                                                     vsl * 512 + sft * 128 + 128],
                                       p_sb[:, sft * 512:(sft + 1) * 512],
                                       start=(first and sft == 0),
                                       stop=(last and sft == 3)),
                                   waits=[t_v] +
                                         ([o_free] if first and sft == 0 else []),
                                   incs=[(MM, 1)])
                            pfree[sft] = (MM, MM.n)
                        kfree[ksl] = (MM, MM.n)
                        vfree[vsl] = (MM, MM.n)
                    qfree[qsl] = (MM, MM.n)
                    mm_attn = MM.n
                    negc_free = (ACT, ACT.n)
                    p.emit("vector", lambda v: v.reciprocal(
                        linv_sb[:], B[2][:]),
                        waits=[(MM, mm_attn)], incs=[(VE, 1)])
                    l_free = (VE, VE.n)
                    p.emit("vector", lambda v, h=h: v.tensor_mul(
                        oT_sb[:, h * 512:(h + 1) * 512], B[3][:], linv_sb[:]),
                        waits=[oT_free[h]], incs=[(VE, 1)])
                    o_free = (VE, VE.n)
                ve_oT = VE.n
                for tt in range(4):
                    for db in range(4):
                        wsl = (tt * 4 + db) % 2
                        bank = B[6 + wsl]
                        for dc in range(2):
                            p.emit("tensor", lambda t, tt=tt, db=db, dc=dc,
                                   bank=bank: t.matmul(
                                       bank[:],
                                       oT_sb[:, dc * 512 + tt * 128:
                                             dc * 512 + tt * 128 + 128],
                                       wo_sb[:, dc * D + db * 512:
                                             dc * D + db * 512 + 512],
                                       start=(dc == 0), stop=(dc == 1)),
                                   waits=([(VE, ve_oT), t_wo, wo_bfree[wsl]]
                                          if dc == 0 else []))
                        p.inc_last("tensor", MM, 1)
                        mm_wo = MM.n
                        p.emit("vector", lambda v, wsl=wsl, bank=bank:
                               v.tensor_copy(out_sb[:, wsl * 512:(wsl + 1) * 512],
                                             bank[:]),
                               waits=[(MM, mm_wo), wo_outfree[wsl]],
                               incs=[(VE, 1)])
                        wo_bfree[wsl] = (VE, VE.n)
                        p.emit("sync", dma(
                            attn_part[qc * 512 + tt * 128:
                                      qc * 512 + tt * 128 + 128,
                                      db * 512:(db + 1) * 512],
                            out_sb[:, wsl * 512:(wsl + 1) * 512]),
                            waits=[(VE, VE.n)], incs=[(out_st[wsl], 16)])
                        wo_outfree[wsl] = out_st[wsl].tok()
                oT_free[0] = (MM, MM.n)
                oT_free[1] = (MM, MM.n)
            outfree[0] = wo_outfree[0]
            outfree[1] = wo_outfree[1]


        if phases >= 5:
            p.emit("gpsimd", lambda g: g.collective_compute(
                "ReduceScatter", ALU.add, replica_groups=[list(range(R))],
                ins=[attn_part[:]], outs=[attn_own[:]]),
                waits=all_store_toks(), incs=[(CC, 1)])
            cc_rs_done = CC.n

            at_free = None
            xs_free = None
            x2_free = outfree[0]
            for t8 in range(8):
                for hg in range(4):
                    p.emit("sync", dma(
                        xb_sb[:, 0:512],
                        attn_own[t8 * 128:(t8 + 1) * 128,
                                 hg * 512:(hg + 1) * 512]),
                        waits=[(CC, cc_rs_done), at_free, xfree[0]],
                        incs=[(xb_ld[0], 16)])
                    xfree[0] = None
                    t_at = xb_ld[0].tok()
                    p.emit("sync", dma(
                        xr_sb[:, 0:512].rearrange("p (hc t) -> p hc t", t=128),
                        x_dram[:, hg * 4:(hg + 1) * 4,
                               t8 * 128:(t8 + 1) * 128]),
                        waits=[xs_free, xrfree[0]], incs=[(xr_ld, 16)])
                    xrfree[0] = None
                    t_xs = xr_ld.tok()
                    for hc4 in range(4):
                        p.emit("tensor", lambda t, hc4=hc4: t.transpose(
                            B[0][:, hc4 * 128:(hc4 + 1) * 128],
                            xb_sb[:, hc4 * 128:(hc4 + 1) * 128], ident[:]),
                            waits=([t_at, bank_free.get(0)]
                                   if hc4 == 0 else []))
                    p.inc_last("tensor", MM, 1)
                    at_free = (MM, MM.n)
                    p.emit("vector", lambda v: v.tensor_add(
                        out_sb[:, 0:512], B[0][:], xr_sb[:, 0:512]),
                        waits=[(MM, MM.n), t_xs, x2_free], incs=[(VE, 1)])
                    bank_free[0] = (VE, VE.n)
                    xs_free = (VE, VE.n)
                    p.emit("sync", dma(
                        x_dram[:, hg * 4:(hg + 1) * 4,
                               t8 * 128:(t8 + 1) * 128],
                        out_sb[:, 0:512].rearrange("p (hc t) -> p hc t", t=128)),
                        waits=[(VE, VE.n)], incs=[(out_st[0], 16)])
                    x2_free = out_st[0].tok()
            outfree[0] = x2_free


        if phases >= 6:
            bank_free.clear()
            ffn_pass(x_dram, yT, first_wait=x2_free)


        if probe == "tabs":
            p.emit("sync", dma(yT[0:1, 0, 0:32], negCs[:]),
                   waits=all_store_toks() + [(CC, CC.n), (MM, MM.n),
                                             (VE, VE.n), (ACT, ACT.n)],
                   incs=[(out_st[3], 16)])
            p.emit("sync", dma(yT[0:1, 0, 32:64], maxQn[:]),
                   incs=[(out_st[3], 16)])
            p.emit("sync", dma(yT[0:1, 0, 64:66], maxKn[:]),
                   incs=[(out_st[3], 16)])
        elif probe is not None:
            yflat = yT.rearrange("p c t -> p (c t)")
            srcs = {
                "x_dram": x_dram.rearrange("p c t -> p (c t)"),
                "cc_xn_in": cc_xn_in[0][:],
                "qd": qd.rearrange("h p s -> p h s"),
                "kd": kd.rearrange("h p s -> p h s"),
                "vtm": vtm.rearrange("(a p) d -> p a d", p=128),
                "part0": attn_part[0:1024].rearrange("(a p) d -> p a d", p=128),
                "attn_own": attn_own.rearrange("(a p) d -> p a d", p=128),
            }
            sp = srcs[probe]
            yv = yflat if len(sp.shape) == 2 else yflat.rearrange(
                "p (a d) -> p a d", d=sp.shape[-1])
            p.emit("sync", dma(yv, sp),
                   waits=all_store_toks() + [(CC, CC.n), (MM, MM.n),
                                             (VE, VE.n), (ACT, ACT.n)],
                   incs=[(out_st[3], 16)])

        p.emit("sync", None,
               waits=all_store_toks() + [(MM, MM.n), (VE, VE.n),
                                         (ACT, ACT.n), (CC, CC.n),
                                         (GP, GP.n)])

        p.finalize(block)

    return nc


def kernel(L0_attn_x, L0_attn_Wq, L0_attn_Wk, L0_attn_Wv, L0_attn_Wo,
           L0_ffn_gate_w, L0_ffn_up_w, L0_ffn_down_w,
           L0_ffn_gate_s, L0_ffn_up_s, L0_ffn_down_s):
    x = np.ascontiguousarray(np.asarray(L0_attn_x, np.float32)[0])
    Wq = np.asarray(L0_attn_Wq, np.float32)
    Wk = np.asarray(L0_attn_Wk, np.float32)
    Wv = np.asarray(L0_attn_Wv, np.float32)
    Wo = np.asarray(L0_attn_Wo, np.float32)
    gwf = np.asarray(L0_ffn_gate_w, np.float32)
    uwf = np.asarray(L0_ffn_up_w, np.float32)
    dwf = np.asarray(L0_ffn_down_w, np.float32)
    gsf = np.asarray(L0_ffn_gate_s, np.float32)
    usf = np.asarray(L0_ffn_up_s, np.float32)
    dsf = np.asarray(L0_ffn_down_s, np.float32)

    g_host = np.ascontiguousarray(
        gwf.reshape(IK, 128, HK, 128).transpose(3, 0, 2, 1))
    u_host = np.ascontiguousarray(
        uwf.reshape(IK, 128, HK, 128).transpose(3, 0, 2, 1))
    d_host = np.ascontiguousarray(
        dwf.reshape(4, 512, IK, 128).transpose(3, 2, 0, 1))
    gs_host = np.ascontiguousarray(gsf.reshape(IK, 128).T)
    us_host = np.ascontiguousarray(usf.reshape(IK, 128).T)
    ds_host = np.ascontiguousarray(dsf.reshape(HK, 128).T)

    nc = build()
    in_maps = []
    for r in range(R):
        xs = x[TL * r:TL * (r + 1)].T
        x_host = np.ascontiguousarray(
            xs.reshape(HK, 128, TL).transpose(1, 0, 2))
        wq_r = np.ascontiguousarray(
            Wq[:, 256 * r:256 * (r + 1)].reshape(HK, 128, 256)
            .transpose(1, 0, 2))
        wk_r = np.ascontiguousarray(
            Wk[:, 256 * r:256 * (r + 1)].reshape(HK, 128, 256)
            .transpose(1, 0, 2))
        wv_r = np.ascontiguousarray(
            Wv[:, 256 * r:256 * (r + 1)].reshape(HK, 128, 256)
            .transpose(1, 0, 2))
        wo_r = np.ascontiguousarray(
            Wo[256 * r:256 * (r + 1), :].reshape(2, 128, D)
            .transpose(1, 0, 2))
        in_maps.append({
            "xT": x_host, "wq": wq_r, "wk": wk_r, "wv": wv_r, "wo": wo_r,
            "gw": g_host, "uw": u_host, "dw": d_host,
            "gs": gs_host, "us": us_host, "ds": ds_host,
        })

    res = run_bass_kernel_spmd(nc, in_maps, core_ids=list(range(R)))

    y = np.empty((1, S, D), np.float32)
    for r in range(R):
        yT = np.asarray(res.results[r]["yT"])
        y[0, TL * r:TL * (r + 1), :] = (
            yT.transpose(2, 1, 0).reshape(TL, D))
    return y


# version 10
# speedup vs baseline: 1.0906x; 1.0906x over previous
from contextlib import ExitStack

import numpy as np

import concourse.bass as bass
import concourse.mybir as mybir
from concourse.bass_utils import run_bass_kernel_spmd

F32 = mybir.dt.float32
BF16 = mybir.dt.bfloat16
AF = mybir.ActivationFunctionType
ALU = mybir.AluOpType
AX = mybir.AxisListType

R = 8
D = 2048
S = 8192
TL = 1024
HK = D // 128
IK = 44
INTER = 5632
NHL = 2
SCALE = float(1.0 / np.sqrt(128.0))
EPS = 1e-5
NEG = -1.0e30
INV_D = float(1.0 / D)


class Ctr:

    def __init__(self, sem):
        self.sem = sem
        self.n = 0

    def tok(self):
        return (self, self.n)


class Prog:

    ENGINES = ("sync", "gpsimd", "tensor", "vector", "scalar")

    def __init__(self):
        self.items = {e: [] for e in self.ENGINES}

    def emit(self, eng, fn, waits=(), incs=()):
        rec = []
        for c, k in incs:
            c.n += k
            rec.append((c, k))
        self.items[eng].append((fn, [w for w in waits if w is not None], rec))

    def inc_last(self, eng, ctr, k=1):
        ctr.n += k
        self.items[eng][-1][2].append((ctr, k))

    def finalize(self, block):
        def run(eng_obj, eng_name):
            last_wait = {}
            for fn, waits, incs in self.items[eng_name]:
                for ctr, val in waits:
                    if val <= 0:
                        continue
                    if last_wait.get(id(ctr), -1) >= val:
                        continue
                    eng_obj.wait_ge(ctr.sem, val)
                    last_wait[id(ctr)] = val
                if fn is None:
                    continue
                instr = fn(eng_obj)
                for ctr, k in incs:
                    instr.then_inc(ctr.sem, k)

        if self.items["sync"]:
            block.sync(lambda e: run(e, "sync"))
        if self.items["gpsimd"]:
            block.gpsimd(lambda e: run(e, "gpsimd"))
        if self.items["tensor"]:
            block.tensor(lambda e: run(e, "tensor"))
        if self.items["vector"]:
            block.vector(lambda e: run(e, "vector"))
        if self.items["scalar"]:
            block.scalar(lambda e: run(e, "scalar"))


def build(dbg=False, phases=6, probe=None):
    nc = bass.Bass("TRN2", target_bir_lowering=False,
                   detect_race_conditions=False)
    KD = "ExternalOutput" if dbg else "Internal"

    xT = nc.dram_tensor("xT", [128, HK, TL], F32, kind="ExternalInput").ap()
    wq = nc.dram_tensor("wq", [128, HK, 256], F32, kind="ExternalInput").ap()
    wk = nc.dram_tensor("wk", [128, HK, 256], F32, kind="ExternalInput").ap()
    wv = nc.dram_tensor("wv", [128, HK, 256], F32, kind="ExternalInput").ap()
    wo = nc.dram_tensor("wo", [128, 2, D], F32, kind="ExternalInput").ap()
    gw = nc.dram_tensor("gw", [128, IK, HK, 128], F32, kind="ExternalInput").ap()
    uw = nc.dram_tensor("uw", [128, IK, HK, 128], F32, kind="ExternalInput").ap()
    dw = nc.dram_tensor("dw", [128, IK, 4, 512], F32, kind="ExternalInput").ap()
    gs = nc.dram_tensor("gs", [128, IK], F32, kind="ExternalInput").ap()
    us = nc.dram_tensor("us", [128, IK], F32, kind="ExternalInput").ap()
    ds = nc.dram_tensor("ds", [128, HK], F32, kind="ExternalInput").ap()

    yT = nc.dram_tensor("yT", [128, HK, TL], F32, kind="ExternalOutput").ap()

    x_dram = nc.dram_tensor("x_dram", [128, HK, TL], F32, kind=KD).ap()
    cc_xn_in = [nc.dram_tensor(f"cc_xn_in{g}", [128, HK * 512], F32,
                               kind=KD).ap() for g in range(2)]
    cc_xn_out = [nc.dram_tensor(f"cc_xn_out{g}", [R * 128, HK * 512],
                                F32).ap() for g in range(2)]
    qd = nc.dram_tensor("qd", [NHL, 128, S], F32, kind=KD).ap()
    kd = nc.dram_tensor("kd", [NHL, 128, S], F32, kind=KD).ap()
    vtm = nc.dram_tensor("vtm", [S, 256], F32, kind=KD).ap()
    attn_part = nc.dram_tensor("attn_part", [S, D], F32, kind=KD).ap()
    attn_own = nc.dram_tensor("attn_own", [TL, D], F32, kind=KD).ap()

    p = Prog()

    with ExitStack() as stk:
        e = stk.enter_context

        ones_col = e(nc.sbuf_tensor("ones_col", [1, 128], F32))
        ones128 = e(nc.sbuf_tensor("ones128", [128, 1], F32))
        ones_sq = e(nc.sbuf_tensor("ones_sq", [128, 128], F32))
        ident = e(nc.sbuf_tensor("ident", [128, 128], F32))
        masks = e(nc.sbuf_tensor("masks", [128, 4 * 512], F32))
        gs_sb = e(nc.sbuf_tensor("gs_sb", [128, IK], F32))
        us_sb = e(nc.sbuf_tensor("us_sb", [128, IK], F32))
        dsh_sb = e(nc.sbuf_tensor("dsh_sb", [128, HK], F32))
        maxQn = e(nc.sbuf_tensor("maxQn", [1, NHL * 16], F32))
        maxKn = e(nc.sbuf_tensor("maxKn", [1, NHL], F32))
        negCs = e(nc.sbuf_tensor("negCs", [1, NHL * 16], F32))
        rows = e(nc.sbuf_tensor("rows", [1, 6 * 512], F32))

        xb_sb = e(nc.sbuf_tensor("xb_sb", [128, 2 * 512], F32))
        sq_sb = e(nc.sbuf_tensor("sq_sb", [128, 512], F32))
        tmp_sb = e(nc.sbuf_tensor("tmp_sb", [128, 512], F32))
        out_sb = e(nc.sbuf_tensor("out_sb", [128, 2 * 512], F32))
        xr_sb = e(nc.sbuf_tensor("xr_sb", [128, 2 * 512], F32))
        negc_sb = e(nc.sbuf_tensor("negc_sb", [128, 1], F32))

        arena = e(nc.sbuf_tensor("arena", [128, 43008], F32))
        A = arena
        xnf_sb = A[:, 0:512]
        r1f_sb = A[:, 512:1024]
        sg_sb = A[:, 1536:2048]
        hdn_sb = A[:, 2048:24576]
        xnp_sb = A[:, 24576:36864].bitcast(BF16)
        gbf_sb = A[:, 36864:40960].bitcast(BF16)
        dbf_sb = A[:, 40960:41472].bitcast(BF16)
        hsp_sb = A[:, 41472:43008].bitcast(BF16)
        xnt_sb = A[:, 0:16384]
        wq_sb = A[:, 16384:20480]
        wk_sb = A[:, 20480:24576]
        wv_sb = A[:, 24576:28672]
        ev_sb = A[:, 28672:32768]
        q_sb = A[:, 0:1024]
        k_sb = A[:, 1024:3072]
        v_sb = A[:, 3072:5120]
        p_sb = A[:, 5120:7168]
        linv_sb = A[:, 7168:7680]
        oT_sb = A[:, 7680:8704]
        wo_sb = A[:, 8704:12800]

        B = [e(nc.psum_tensor(f"pb{i}", [128, 512], F32)) for i in range(8)]

        def C(name):
            return Ctr(e(nc.semaphore(name)))

        MM, VE, ACT, GP, CC = C("s_mm"), C("s_ve"), C("s_ac"), C("s_gp"), \
            C("s_cc")
        sc_ld = C("sc_ld")
        wqkv_ld = C("wqkv_ld")
        wo_ld = C("wo_ld")
        xb_ld = [C("xb_ld0"), C("xb_ld1")]
        wgu_ld = [C("wgu_ld0"), C("wgu_ld1")]
        wd_ld = [C("wd_ld0"), C("wd_ld1")]
        xr_ld = C("xr_ld")
        xnt_ld = [C("xnt_ld0"), C("xnt_ld1")]
        k_ld = [C(f"k_ld{i}") for i in range(4)]
        v_ld = [C(f"v_ld{i}") for i in range(4)]
        q_ld = C("q_ld")
        out_st = [C("out_st0"), C("out_st1"), C("out_st2"), C("out_st3")]
        ev_st = [C("ev_st0"), C("ev_st1"), C("ev_st2"), C("ev_st3")]
        vt_st = [C("vt_st0"), C("vt_st1")]

        block = e(nc.Block())

        def dma(dst, src):
            return lambda g: g.dma_start(out=dst, in_=src)

        def all_store_toks():
            return [s.tok() for s in out_st + ev_st + vt_st]

        p.emit("sync", dma(gs_sb[:], gs), incs=[(sc_ld, 16)])
        p.emit("sync", dma(us_sb[:], us), incs=[(sc_ld, 16)])
        p.emit("sync", dma(dsh_sb[:], ds), incs=[(sc_ld, 16)])
        t_scales = sc_ld.tok()

        p.emit("vector", lambda v: v.memset(ones_col[:], 1.0), incs=[(VE, 1)])
        p.emit("vector", lambda v: v.memset(ones128[:], 1.0), incs=[(VE, 1)])
        p.emit("vector", lambda v: v.memset(ones_sq[:], 1.0), incs=[(VE, 1)])
        p.emit("vector", lambda v: v.memset(maxKn[:], 0.0), incs=[(VE, 1)])
        p.emit("vector", lambda v: v.tensor_scalar(
            dsh_sb[:], dsh_sb[:], 0.5, None, ALU.mult),
            waits=[t_scales], incs=[(VE, 1)])
        ve_const = VE.n

        p.emit("gpsimd", lambda g: g.memset(ident[:], 0.0), incs=[(GP, 1)])
        p.emit("gpsimd", lambda g: g.affine_select(
            out=ident[:], in_=ident[:], compare_op=ALU.not_equal, fill=1.0,
            base=0, pattern=[[-1, 128]], channel_multiplier=1),
            waits=[(GP, GP.n)], incs=[(GP, 1)])
        for sft in range(4):
            p.emit("gpsimd", lambda g, sft=sft: g.memset(
                masks[:, sft * 512:(sft + 1) * 512], 0.0), incs=[(GP, 1)])
            p.emit("gpsimd", lambda g, sft=sft: g.affine_select(
                out=masks[:, sft * 512:(sft + 1) * 512],
                in_=masks[:, sft * 512:(sft + 1) * 512],
                compare_op=ALU.is_ge, fill=NEG,
                base=-128 * sft, pattern=[[1, 512]], channel_multiplier=-1),
                waits=[(GP, GP.n)], incs=[(GP, 1)])
        gp_const = GP.n

        bank_free = {}
        xfree = [None, None]
        sqfree = [None]
        outfree = [None] * 2
        xrfree = [None] * 2

        def ln_stats(src, c0, c1, extra_first_wait=None):
            for hc in range(HK):
                sl = hc % 2
                p.emit("sync", dma(xb_sb[:, sl * 512:(sl + 1) * 512],
                                   src[:, hc, c0:c1]),
                       waits=[xfree[sl]] +
                             ([extra_first_wait] if hc == 0 else []),
                       incs=[(xb_ld[sl], 16)])
                t_x = xb_ld[sl].tok()
                p.emit("tensor", lambda t, sl=sl, hc=hc: t.matmul(
                    B[0][:], ones_sq[:],
                    xb_sb[:, sl * 512:(sl + 1) * 512],
                    start=(hc == 0), stop=(hc == HK - 1)),
                    waits=[t_x, (VE, ve_const)] +
                          ([bank_free.get(0)] if hc == 0 else []))
                p.emit("vector", lambda v, sl=sl: v.tensor_mul(
                    sq_sb[:],
                    xb_sb[:, sl * 512:(sl + 1) * 512],
                    xb_sb[:, sl * 512:(sl + 1) * 512]),
                    waits=[t_x, sqfree[0]], incs=[(VE, 1)])
                ve_sq = VE.n
                p.emit("tensor", lambda t, hc=hc: t.matmul(
                    B[1][:], ones_sq[:],
                    sq_sb[:],
                    start=(hc == 0), stop=(hc == HK - 1)),
                    waits=[(VE, ve_sq)] +
                          ([bank_free.get(1)] if hc == 0 else []),
                    incs=[(MM, 1)])
                xfree[sl] = (MM, MM.n)
                sqfree[0] = (MM, MM.n)
            return MM.n

        def ln_rows(mm_stats):
            mrow = rows[:, 0:512]
            srow = rows[:, 512:1024]
            m2row = rows[:, 1024:1536]
            vrow = rows[:, 1536:2048]
            rsrow = rows[:, 2048:2560]
            mrsrow = rows[:, 2560:3072]
            p.emit("vector", lambda v: v.tensor_scalar(
                mrow, B[0][0:1, :], INV_D, None, ALU.mult),
                waits=[(MM, mm_stats)])
            p.emit("vector", lambda v: v.tensor_scalar(
                srow, B[1][0:1, :], INV_D, None, ALU.mult))
            p.emit("vector", lambda v: v.tensor_mul(m2row, mrow, mrow))
            p.emit("vector", lambda v: v.tensor_sub(vrow, srow, m2row))
            p.emit("vector", lambda v: v.tensor_single_scalar(
                vrow, vrow, EPS, ALU.add), incs=[(VE, 1)])
            p.emit("scalar", lambda s: s.activation(
                srow, vrow, AF.Sqrt), waits=[(VE, VE.n)], incs=[(ACT, 1)])
            p.emit("vector", lambda v: v.reciprocal(rsrow, srow),
                   waits=[(ACT, ACT.n)])
            p.emit("vector", lambda v: v.tensor_mul(mrsrow, mrow, rsrow),
                   incs=[(VE, 1)])
            ve_rows = VE.n
            p.emit("tensor", lambda t: t.matmul(
                B[2][:], ones_col[:], rsrow, start=True, stop=True),
                waits=[(VE, ve_rows), bank_free.get(2)])
            p.emit("tensor", lambda t: t.matmul(
                B[3][:], ones_col[:], mrsrow, start=True, stop=True),
                waits=[bank_free.get(3)], incs=[(MM, 1)])
            return MM.n

        def ffn_pass(src, dst, first_wait=None):
            hdn_ready = [None] * IK
            wfree = [None, None]
            dfree = [None, None]
            gb_free = [None, None]
            sg_free = [None]
            obfree = [None] * 4
            hspfree = [None, None]
            xnp_free = [None]
            xh = xnp_sb[:, 0:8192]
            xm = xnp_sb[:, 8192:16384]
            xl = xnp_sb[:, 16384:24576]
            for q5 in range(2):
                c0, c1 = q5 * 512, (q5 + 1) * 512
                mm_stats = ln_stats(src, c0, c1,
                                    first_wait if q5 == 0 else None)
                mm_bcast = ln_rows(mm_stats)
                for hc in range(HK):
                    sl = hc % 2
                    s5 = slice(hc * 512, (hc + 1) * 512)
                    p.emit("sync", dma(xb_sb[:, sl * 512:(sl + 1) * 512],
                                       src[:, hc, c0:c1]),
                           waits=[xfree[sl]], incs=[(xb_ld[sl], 16)])
                    t_x = xb_ld[sl].tok()
                    p.emit("vector", lambda v, sl=sl: v.tensor_mul(
                        tmp_sb[:], xb_sb[:, sl * 512:(sl + 1) * 512], B[2][:]),
                        waits=[t_x, (MM, mm_bcast)] +
                              ([xnp_free[0]] if hc == 0 else []))
                    p.emit("vector", lambda v: v.tensor_sub(
                        xnf_sb[:], tmp_sb[:], B[3][:]))
                    p.emit("vector", lambda v, s5=s5: v.tensor_copy(
                        xh[:, s5], xnf_sb[:]))
                    p.emit("vector", lambda v, s5=s5: v.tensor_sub(
                        xm[:, s5], xnf_sb[:], xh[:, s5]))
                    p.emit("vector", lambda v, s5=s5: v.tensor_sub(
                        r1f_sb[:], xnf_sb[:], xh[:, s5]))
                    p.emit("vector", lambda v, s5=s5: v.tensor_sub(
                        xl[:, s5], r1f_sb[:], xm[:, s5]), incs=[(VE, 1)])
                    xfree[sl] = (VE, VE.n)
                ve_xn = VE.n
                bank_free[2] = (VE, ve_xn)
                bank_free[3] = (VE, ve_xn)

                for isl in range(IK):
                    sl = isl % 2
                    gslab = gbf_sb[:, (2 * sl) * 2048:(2 * sl + 1) * 2048]
                    uslab = gbf_sb[:, (2 * sl + 1) * 2048:(2 * sl + 2) * 2048]
                    p.emit("gpsimd", dma(
                        gslab.rearrange("p (c n) -> p c n", n=128),
                        gw[:, isl]), waits=[wfree[sl]],
                        incs=[(wgu_ld[sl], 16)])
                    p.emit("gpsimd", dma(
                        uslab.rearrange("p (c n) -> p c n", n=128),
                        uw[:, isl]), incs=[(wgu_ld[sl], 16)])
                    t_w = wgu_ld[sl].tok()
                    gbank, ubank = B[2 * sl], B[2 * sl + 1]
                    for wslab, bank, gfree in ((gslab, gbank, gb_free[sl]),
                                               (uslab, ubank, None)):
                        first = True
                        for c in range(HK):
                            for xp in (xh, xm, xl):
                                p.emit("tensor", lambda t, wslab=wslab, c=c,
                                       bank=bank, xp=xp: t.matmul(
                                           bank[:],
                                           wslab[:, c * 128:(c + 1) * 128],
                                           xp[:, c * 512:(c + 1) * 512],
                                           start=(c == 0 and xp is xh),
                                           stop=(c == HK - 1 and xp is xl)),
                                       waits=([t_w, (VE, ve_xn), gfree]
                                              if first else []))
                                first = False
                        p.inc_last("tensor", MM, 1)
                        if wslab is gslab:
                            mm_gate = MM.n
                        else:
                            mm_up = MM.n
                    wfree[sl] = (MM, MM.n)
                    p.emit("scalar", lambda s, gbank=gbank, isl=isl:
                           s.activation(sg_sb[:], gbank[:], AF.Silu,
                                        scale=gs_sb[:, isl:isl + 1]),
                           waits=[(MM, mm_gate), sg_free[0], t_scales],
                           incs=[(ACT, 1)])
                    ac_silu = ACT.n
                    p.emit("vector", lambda v, ubank=ubank, isl=isl:
                           v.scalar_tensor_tensor(
                               hdn_sb[:, isl * 512:(isl + 1) * 512], ubank[:],
                               us_sb[:, isl:isl + 1], sg_sb[:],
                               ALU.mult, ALU.mult),
                           waits=[(MM, mm_up), (ACT, ac_silu)], incs=[(VE, 1)])
                    sg_free[0] = (VE, VE.n)
                    gb_free[sl] = (VE, VE.n)
                    hdn_ready[isl] = (VE, VE.n)
                xnp_free[0] = (MM, MM.n)

                for hg in range(4):
                    t_xr = [None, None]
                    for hc4 in range(2):
                        p.emit("sync", dma(
                            xr_sb[:, hc4 * 512:(hc4 + 1) * 512],
                            src[:, hg * 4 + hc4, c0:c1]),
                            waits=[xrfree[hc4]], incs=[(xr_ld, 16)])
                        xrfree[hc4] = None
                        t_xr[hc4] = xr_ld.tok()
                    for ic in range(IK):
                        sl = ic % 2
                        dslab = dbf_sb[:, sl * 512:(sl + 1) * 512]
                        p.emit("gpsimd", dma(dslab, dw[:, ic, hg]),
                               waits=[dfree[sl]], incs=[(wd_ld[sl], 16)])
                        t_d = wd_ld[sl].tok()
                        hs = hsp_sb[:, sl * 1536:(sl + 1) * 1536]
                        hh, hm, hl = hs[:, 0:512], hs[:, 512:1024], \
                            hs[:, 1024:1536]
                        hdn_ic = hdn_sb[:, ic * 512:(ic + 1) * 512]
                        p.emit("vector", lambda v, hh=hh, hdn_ic=hdn_ic:
                               v.tensor_copy(hh, hdn_ic),
                               waits=[hdn_ready[ic], hspfree[sl]])
                        p.emit("vector", lambda v, hm=hm, hdn_ic=hdn_ic,
                               hh=hh: v.tensor_sub(hm, hdn_ic, hh))
                        p.emit("vector", lambda v, hdn_ic=hdn_ic, hh=hh:
                               v.tensor_sub(r1f_sb[:], hdn_ic, hh))
                        p.emit("vector", lambda v, hl=hl, hm=hm:
                               v.tensor_sub(hl, r1f_sb[:], hm),
                               incs=[(VE, 1)])
                        ve_hsp = VE.n
                        first = True
                        for hc4 in range(4):
                            for pi_, hp in enumerate((hh, hm, hl)):
                                st_ = (ic == 0 and pi_ == 0)
                                sp_ = (ic == IK - 1 and pi_ == 2)
                                p.emit("tensor", lambda t, dslab=dslab,
                                       hc4=hc4, hp=hp, st_=st_, sp_=sp_:
                                       t.matmul(
                                           B[4 + hc4][:],
                                           dslab[:, hc4 * 128:(hc4 + 1) * 128],
                                           hp, start=st_, stop=sp_),
                                       waits=([t_d, (VE, ve_hsp)] +
                                              ([obfree[0], obfree[1],
                                                obfree[2], obfree[3]]
                                               if ic == 0 else [])
                                              if first else []))
                                first = False
                        p.inc_last("tensor", MM, 1)
                        dfree[sl] = (MM, MM.n)
                        hspfree[sl] = (MM, MM.n)
                    mm_down = MM.n
                    for hc4 in range(4):
                        hcg = hg * 4 + hc4
                        sl2 = hc4 % 2
                        if hc4 >= 2:
                            p.emit("sync", dma(
                                xr_sb[:, sl2 * 512:(sl2 + 1) * 512],
                                src[:, hcg, c0:c1]),
                                waits=[xrfree[sl2]], incs=[(xr_ld, 16)])
                            t_xr[sl2] = xr_ld.tok()
                        p.emit("vector", lambda v, sl2=sl2, hc4=hc4, hcg=hcg:
                               v.scalar_tensor_tensor(
                                   out_sb[:, sl2 * 512:(sl2 + 1) * 512],
                                   B[4 + hc4][:], dsh_sb[:, hcg:hcg + 1],
                                   xr_sb[:, sl2 * 512:(sl2 + 1) * 512],
                                   ALU.mult, ALU.add),
                               waits=[(MM, mm_down), t_xr[sl2],
                                      outfree[sl2]],
                               incs=[(VE, 1)])
                        obfree[hc4] = (VE, VE.n)
                        xrfree[sl2] = (VE, VE.n)
                        p.emit("sync", dma(
                            dst[:, hcg, c0:c1],
                            out_sb[:, sl2 * 512:(sl2 + 1) * 512]),
                            waits=[(VE, VE.n)], incs=[(out_st[sl2], 16)])
                        outfree[sl2] = out_st[sl2].tok()

        ffn_pass(xT, x_dram)
        barrier_p1 = all_store_toks()

        if phases >= 2:
            onfree = [None, None]
            for q5 in range(2):
                c0, c1 = q5 * 512, (q5 + 1) * 512
                mm_stats = ln_stats(x_dram, c0, c1,
                                    barrier_p1[0] if q5 == 0 else None)
                mm_bcast = ln_rows(mm_stats)
                for hc in range(HK):
                    sl = hc % 2
                    p.emit("sync", dma(xb_sb[:, sl * 512:(sl + 1) * 512],
                                       x_dram[:, hc, c0:c1]),
                           waits=[xfree[sl]], incs=[(xb_ld[sl], 16)])
                    t_x = xb_ld[sl].tok()
                    p.emit("vector", lambda v, sl=sl: v.tensor_mul(
                        tmp_sb[:], xb_sb[:, sl * 512:(sl + 1) * 512], B[2][:]),
                        waits=[t_x, (MM, mm_bcast), onfree[sl]])
                    p.emit("vector", lambda v, sl=sl: v.tensor_sub(
                        out_sb[:, sl * 512:(sl + 1) * 512], tmp_sb[:], B[3][:]),
                        incs=[(VE, 1)])
                    xfree[sl] = (VE, VE.n)
                    p.emit("sync", dma(
                        cc_xn_in[q5][:, hc * 512:(hc + 1) * 512],
                        out_sb[:, sl * 512:(sl + 1) * 512]),
                        waits=[(VE, VE.n)], incs=[(out_st[sl], 16)])
                    onfree[sl] = out_st[sl].tok()
                    outfree[sl] = out_st[sl].tok()
                bank_free[2] = (VE, VE.n)
                bank_free[3] = (VE, VE.n)
                p.emit("gpsimd", lambda g, q5=q5: g.collective_compute(
                    "AllGather", ALU.bypass,
                    replica_groups=[list(range(R))],
                    ins=[cc_xn_in[q5][:]], outs=[cc_xn_out[q5][:]]),
                    waits=[out_st[0].tok(), out_st[1].tok()], incs=[(CC, 1)])
            cc_xn_done = CC.n


        if phases >= 3:
            p.emit("sync", dma(wq_sb.rearrange("p (c n) -> p c n", n=256), wq),
                   waits=[(CC, cc_xn_done - 1)], incs=[(wqkv_ld, 16)])
            p.emit("sync", dma(wk_sb.rearrange("p (c n) -> p c n", n=256), wk),
                   incs=[(wqkv_ld, 16)])
            p.emit("sync", dma(wv_sb.rearrange("p (c n) -> p c n", n=256), wv),
                   incs=[(wqkv_ld, 16)])
            t_wqkv = wqkv_ld.tok()
            xn_view = [cco.rearrange("(r p) (c t) -> r p c t", p=128, t=512)
                       for cco in cc_xn_out]
            xnt_free = [None, None]
            ev_free = [None] * 8
            b7_free = None
            for ni, nt in enumerate(list(range(0, 16, 2)) +
                                    list(range(1, 16, 2))):
                blk = nt // 2
                q5 = nt % 2
                sl = ni % 2
                xnt = xnt_sb[:, sl * HK * 512:(sl + 1) * HK * 512]
                p.emit("sync", dma(
                    xnt.rearrange("p (c t) -> p c t", t=512),
                    xn_view[q5][blk, :, :, :]),
                    waits=[xnt_free[sl], (CC, cc_xn_done - 1 + q5)],
                    incs=[(xnt_ld[sl], 16)])
                t_xnt = xnt_ld[sl].tok()
                for pi, wsb in ((0, wq_sb), (1, wk_sb), (2, wv_sb)):
                    for dc in range(2):
                        bank = B[pi * 2 + dc]
                        for c in range(HK):
                            p.emit("tensor", lambda t, wsb=wsb, c=c, dc=dc,
                                   bank=bank, xnt=xnt: t.matmul(
                                       bank[:],
                                       wsb[:, c * 256 + dc * 128:
                                           c * 256 + dc * 128 + 128],
                                       xnt[:, c * 512:(c + 1) * 512],
                                       start=(c == 0), stop=(c == HK - 1)),
                                   waits=([t_xnt, t_wqkv,
                                           bank_free.get(pi * 2 + dc)]
                                          if c == 0 else []))
                        p.inc_last("tensor", MM, 1)
                        bank_free[pi * 2 + dc] = None
                mm_qkv = MM.n
                xnt_free[sl] = (MM, MM.n)
                for pi, dest in ((0, qd), (1, kd)):
                    for dc in range(2):
                        bank = B[pi * 2 + dc]
                        mm_bank = mm_qkv - 5 + (pi * 2 + dc)
                        esl = pi * 2 + dc
                        ev = ev_sb[:, esl * 512:(esl + 1) * 512]
                        p.emit("vector", lambda v, ev=ev, bank=bank: v.tensor_copy(
                            ev, bank[:]),
                            waits=[(MM, mm_bank), ev_free[esl]], incs=[(VE, 1)])
                        bank_free[pi * 2 + dc] = (VE, VE.n)
                        p.emit("sync", dma(dest[dc, :, nt * 512:nt * 512 + 512],
                                           ev),
                               waits=[(VE, VE.n)], incs=[(ev_st[esl], 16)])
                        st_ev = ev_st[esl].tok()
                        p.emit("vector", lambda v, ev=ev: v.tensor_mul(
                            tmp_sb[:], ev, ev), waits=[b7_free], incs=[(VE, 1)])
                        p.emit("tensor", lambda t: t.matmul(
                            B[7][:], ones_sq[:], tmp_sb[:],
                            start=True, stop=True),
                            waits=[(VE, VE.n)], incs=[(MM, 1)])
                        p.emit("vector", lambda v: v.reduce_max(
                            rows[:, 0:1], B[7][0:1, :], axis=AX.X),
                            waits=[(MM, MM.n)], incs=[(VE, 1)])
                        b7_free = (VE, VE.n)
                        t_red = (VE, VE.n)
                        if pi == 0:
                            p.emit("vector", lambda v, dc=dc, nt=nt: v.tensor_copy(
                                maxQn[0:1, dc * 16 + nt:dc * 16 + nt + 1],
                                rows[:, 0:1]), waits=[t_red], incs=[(VE, 1)])
                        else:
                            p.emit("vector", lambda v, dc=dc: v.tensor_tensor(
                                maxKn[0:1, dc:dc + 1], maxKn[0:1, dc:dc + 1],
                                rows[:, 0:1], ALU.max), waits=[t_red],
                                incs=[(VE, 1)])
                        ev_free[esl] = st_ev
                for dc in range(2):
                    bank = B[4 + dc]
                    mm_bank = mm_qkv - 1 + dc
                    esl = 4 + dc
                    ev = ev_sb[:, esl * 512:(esl + 1) * 512]
                    p.emit("vector", lambda v, ev=ev, bank=bank: v.tensor_copy(
                        ev, bank[:]), waits=[(MM, mm_bank), ev_free[esl]],
                        incs=[(VE, 1)])
                    ve_ev = VE.n
                    bank_free[4 + dc] = (VE, VE.n)
                    for tt in range(4):
                        p.emit("tensor", lambda t, ev=ev, tt=tt: t.transpose(
                            B[6][:, tt * 128:(tt + 1) * 128],
                            ev[:, tt * 128:(tt + 1) * 128], ident[:]),
                            waits=([(VE, ve_ev), (GP, gp_const),
                                    bank_free.get(6)] if tt == 0 else []))
                    p.inc_last("tensor", MM, 1)
                    ev_free[esl] = (MM, MM.n)
                    esl2 = 6 + dc
                    ev2 = ev_sb[:, esl2 * 512:(esl2 + 1) * 512]
                    p.emit("vector", lambda v, ev2=ev2: v.tensor_copy(
                        ev2, B[6][:]), waits=[(MM, MM.n), ev_free[esl2]],
                        incs=[(VE, 1)])
                    bank_free[6] = (VE, VE.n)
                    p.emit("sync", dma(
                        vtm[nt * 512:(nt + 1) * 512,
                            dc * 128:(dc + 1) * 128].rearrange(
                                "(tt p) d -> p tt d", p=128),
                        ev2.rearrange("p (tt d) -> p tt d", d=128)),
                        waits=[(VE, VE.n)], incs=[(vt_st[dc], 16)])
                    ev_free[esl2] = vt_st[dc].tok()
            barrier_qkv = [s.tok() for s in ev_st + vt_st]

            for dc in range(NHL):
                p.emit("vector", lambda v, dc=dc: v.tensor_scalar(
                    negCs[0:1, dc * 16:(dc + 1) * 16],
                    maxQn[0:1, dc * 16:(dc + 1) * 16],
                    maxKn[0:1, dc:dc + 1], None, ALU.mult), incs=[(VE, 1)])
            p.emit("scalar", lambda s: s.activation(negCs[:], negCs[:], AF.Sqrt),
                   waits=[(VE, VE.n)], incs=[(ACT, 1)])
            p.emit("vector", lambda v: v.tensor_scalar(
                negCs[:], negCs[:], -SCALE, None, ALU.mult),
                waits=[(ACT, ACT.n)], incs=[(VE, 1)])
            ve_ctab = VE.n


        if phases >= 4:
            p.emit("sync", dma(wo_sb.rearrange("p (c n) -> p c n", n=D), wo),
                   waits=barrier_qkv, incs=[(wo_ld, 16)])
            t_wo = wo_ld.tok()
            kfree = [None] * 4
            vfree = [None] * 4
            pfree = [None] * 4
            qfree = [None, None]
            l_free = None
            o_free = None
            negc_free = None
            linv_free = None
            oT_free = [None, None]
            wo_bfree = [None, None]
            wo_outfree = [outfree[0], outfree[1]]
            sbank_free = [None, None]
            for qc in range(16):
                for h in range(NHL):
                    qsl = (qc * NHL + h) % 2
                    qv = q_sb[:, qsl * 512:(qsl + 1) * 512]
                    p.emit("sync", dma(qv, qd[h, :, qc * 512:qc * 512 + 512]),
                           waits=[qfree[qsl]] +
                                 (barrier_qkv if qc == 0 and h == 0 else []),
                           incs=[(q_ld, 16)])
                    t_q = q_ld.tok()
                    p.emit("tensor", lambda t, h=h, qc=qc: t.matmul(
                        B[4][:, 0:1], ones_col[:],
                        negCs[0:1, h * 16 + qc:h * 16 + qc + 1],
                        start=True, stop=True),
                        waits=[(VE, ve_ctab), negc_free], incs=[(MM, 1)])
                    p.emit("vector", lambda v: v.tensor_copy(
                        negc_sb[:], B[4][:, 0:1]),
                        waits=[(MM, MM.n)], incs=[(VE, 1)])
                    ve_negc = VE.n
                    for j in range(qc + 1):
                        ksl = j % 4
                        vsl = j % 4
                        p.emit("sync", dma(k_sb[:, ksl * 512:(ksl + 1) * 512],
                                           kd[h, :, j * 512:j * 512 + 512]),
                               waits=[kfree[ksl]], incs=[(k_ld[ksl], 16)])
                        t_k = k_ld[ksl].tok()
                        p.emit("sync", dma(
                            v_sb[:, vsl * 512:(vsl + 1) * 512].rearrange(
                                "p (tt d) -> p tt d", d=128),
                            vtm[j * 512:(j + 1) * 512,
                                h * 128:(h + 1) * 128].rearrange(
                                    "(tt p) d -> p tt d", p=128)),
                            waits=[vfree[vsl]], incs=[(v_ld[vsl], 16)])
                        t_v = v_ld[vsl].tok()
                        diag = (j == qc)
                        first = (j == 0)
                        last = (j == qc)
                        ac_exps = [None] * 4
                        for sft in range(4):
                            sb = sft % 2
                            p.emit("tensor", lambda t, sb=sb, ksl=ksl, sft=sft,
                                   qv=qv: t.matmul(
                                       B[sb][:],
                                       k_sb[:, ksl * 512 + sft * 128:
                                            ksl * 512 + sft * 128 + 128],
                                       qv, start=True, stop=True),
                                   waits=[t_k, t_q, sbank_free[sb]],
                                   incs=[(MM, 1)])
                            mm_s = MM.n
                            if diag:
                                p.emit("vector", lambda v, sb=sb, sft=sft:
                                       v.tensor_add(B[sb][:], B[sb][:],
                                                    masks[:, sft * 512:
                                                          (sft + 1) * 512]),
                                       waits=[(MM, mm_s), (GP, gp_const)],
                                       incs=[(VE, 1)])
                                dep = (VE, VE.n)
                            else:
                                dep = (MM, mm_s)
                            p.emit("scalar", lambda s, sb=sb, sft=sft:
                                   s.activation(
                                       p_sb[:, sft * 512:(sft + 1) * 512],
                                       B[sb][:], AF.Exp, bias=negc_sb[:],
                                       scale=SCALE),
                                   waits=[dep, pfree[sft], (VE, ve_negc)],
                                   incs=[(ACT, 1)])
                            ac_exps[sft] = (ACT, ACT.n)
                            sbank_free[sb] = (ACT, ACT.n)
                        for sft in range(4):
                            p.emit("tensor", lambda t, sft=sft, first=first,
                                   last=last: t.matmul(
                                       B[2][:], ones_sq[:],
                                       p_sb[:, sft * 512:(sft + 1) * 512],
                                       start=(first and sft == 0),
                                       stop=(last and sft == 3)),
                                   waits=[ac_exps[sft]] +
                                         ([l_free] if first and sft == 0 else []))
                            p.emit("tensor", lambda t, sft=sft, vsl=vsl,
                                   first=first, last=last: t.matmul(
                                       B[3][:], v_sb[:, vsl * 512 + sft * 128:
                                                     vsl * 512 + sft * 128 + 128],
                                       p_sb[:, sft * 512:(sft + 1) * 512],
                                       start=(first and sft == 0),
                                       stop=(last and sft == 3)),
                                   waits=[t_v] +
                                         ([o_free] if first and sft == 0 else []),
                                   incs=[(MM, 1)])
                            pfree[sft] = (MM, MM.n)
                        kfree[ksl] = (MM, MM.n)
                        vfree[vsl] = (MM, MM.n)
                    qfree[qsl] = (MM, MM.n)
                    mm_attn = MM.n
                    negc_free = (ACT, ACT.n)
                    p.emit("vector", lambda v: v.reciprocal(
                        linv_sb[:], B[2][:]),
                        waits=[(MM, mm_attn)], incs=[(VE, 1)])
                    l_free = (VE, VE.n)
                    p.emit("vector", lambda v, h=h: v.tensor_mul(
                        oT_sb[:, h * 512:(h + 1) * 512], B[3][:], linv_sb[:]),
                        waits=[oT_free[h]], incs=[(VE, 1)])
                    o_free = (VE, VE.n)
                ve_oT = VE.n
                for tt in range(4):
                    for db in range(4):
                        wsl = (tt * 4 + db) % 2
                        bank = B[6 + wsl]
                        for dc in range(2):
                            p.emit("tensor", lambda t, tt=tt, db=db, dc=dc,
                                   bank=bank: t.matmul(
                                       bank[:],
                                       oT_sb[:, dc * 512 + tt * 128:
                                             dc * 512 + tt * 128 + 128],
                                       wo_sb[:, dc * D + db * 512:
                                             dc * D + db * 512 + 512],
                                       start=(dc == 0), stop=(dc == 1)),
                                   waits=([(VE, ve_oT), t_wo, wo_bfree[wsl]]
                                          if dc == 0 else []))
                        p.inc_last("tensor", MM, 1)
                        mm_wo = MM.n
                        p.emit("vector", lambda v, wsl=wsl, bank=bank:
                               v.tensor_copy(out_sb[:, wsl * 512:(wsl + 1) * 512],
                                             bank[:]),
                               waits=[(MM, mm_wo), wo_outfree[wsl]],
                               incs=[(VE, 1)])
                        wo_bfree[wsl] = (VE, VE.n)
                        p.emit("sync", dma(
                            attn_part[qc * 512 + tt * 128:
                                      qc * 512 + tt * 128 + 128,
                                      db * 512:(db + 1) * 512],
                            out_sb[:, wsl * 512:(wsl + 1) * 512]),
                            waits=[(VE, VE.n)], incs=[(out_st[wsl], 16)])
                        wo_outfree[wsl] = out_st[wsl].tok()
                oT_free[0] = (MM, MM.n)
                oT_free[1] = (MM, MM.n)
            outfree[0] = wo_outfree[0]
            outfree[1] = wo_outfree[1]


        if phases >= 5:
            p.emit("gpsimd", lambda g: g.collective_compute(
                "ReduceScatter", ALU.add, replica_groups=[list(range(R))],
                ins=[attn_part[:]], outs=[attn_own[:]]),
                waits=all_store_toks(), incs=[(CC, 1)])
            cc_rs_done = CC.n

            at_free = None
            xs_free = None
            x2_free = outfree[0]
            for t8 in range(8):
                for hg in range(4):
                    p.emit("sync", dma(
                        xb_sb[:, 0:512],
                        attn_own[t8 * 128:(t8 + 1) * 128,
                                 hg * 512:(hg + 1) * 512]),
                        waits=[(CC, cc_rs_done), at_free, xfree[0]],
                        incs=[(xb_ld[0], 16)])
                    xfree[0] = None
                    t_at = xb_ld[0].tok()
                    p.emit("sync", dma(
                        xr_sb[:, 0:512].rearrange("p (hc t) -> p hc t", t=128),
                        x_dram[:, hg * 4:(hg + 1) * 4,
                               t8 * 128:(t8 + 1) * 128]),
                        waits=[xs_free, xrfree[0]], incs=[(xr_ld, 16)])
                    xrfree[0] = None
                    t_xs = xr_ld.tok()
                    for hc4 in range(4):
                        p.emit("tensor", lambda t, hc4=hc4: t.transpose(
                            B[0][:, hc4 * 128:(hc4 + 1) * 128],
                            xb_sb[:, hc4 * 128:(hc4 + 1) * 128], ident[:]),
                            waits=([t_at, bank_free.get(0)]
                                   if hc4 == 0 else []))
                    p.inc_last("tensor", MM, 1)
                    at_free = (MM, MM.n)
                    p.emit("vector", lambda v: v.tensor_add(
                        out_sb[:, 0:512], B[0][:], xr_sb[:, 0:512]),
                        waits=[(MM, MM.n), t_xs, x2_free], incs=[(VE, 1)])
                    bank_free[0] = (VE, VE.n)
                    xs_free = (VE, VE.n)
                    p.emit("sync", dma(
                        x_dram[:, hg * 4:(hg + 1) * 4,
                               t8 * 128:(t8 + 1) * 128],
                        out_sb[:, 0:512].rearrange("p (hc t) -> p hc t", t=128)),
                        waits=[(VE, VE.n)], incs=[(out_st[0], 16)])
                    x2_free = out_st[0].tok()
            outfree[0] = x2_free


        if phases >= 6:
            bank_free.clear()
            ffn_pass(x_dram, yT, first_wait=x2_free)


        if probe == "tabs":
            p.emit("sync", dma(yT[0:1, 0, 0:32], negCs[:]),
                   waits=all_store_toks() + [(CC, CC.n), (MM, MM.n),
                                             (VE, VE.n), (ACT, ACT.n)],
                   incs=[(out_st[3], 16)])
            p.emit("sync", dma(yT[0:1, 0, 32:64], maxQn[:]),
                   incs=[(out_st[3], 16)])
            p.emit("sync", dma(yT[0:1, 0, 64:66], maxKn[:]),
                   incs=[(out_st[3], 16)])
        elif probe is not None:
            yflat = yT.rearrange("p c t -> p (c t)")
            srcs = {
                "x_dram": x_dram.rearrange("p c t -> p (c t)"),
                "cc_xn_in": cc_xn_in[0][:],
                "qd": qd.rearrange("h p s -> p h s"),
                "kd": kd.rearrange("h p s -> p h s"),
                "vtm": vtm.rearrange("(a p) d -> p a d", p=128),
                "part0": attn_part[0:1024].rearrange("(a p) d -> p a d", p=128),
                "attn_own": attn_own.rearrange("(a p) d -> p a d", p=128),
            }
            sp = srcs[probe]
            yv = yflat if len(sp.shape) == 2 else yflat.rearrange(
                "p (a d) -> p a d", d=sp.shape[-1])
            p.emit("sync", dma(yv, sp),
                   waits=all_store_toks() + [(CC, CC.n), (MM, MM.n),
                                             (VE, VE.n), (ACT, ACT.n)],
                   incs=[(out_st[3], 16)])

        p.emit("sync", None,
               waits=all_store_toks() + [(MM, MM.n), (VE, VE.n),
                                         (ACT, ACT.n), (CC, CC.n),
                                         (GP, GP.n)])

        p.finalize(block)

    return nc


def kernel(L0_attn_x, L0_attn_Wq, L0_attn_Wk, L0_attn_Wv, L0_attn_Wo,
           L0_ffn_gate_w, L0_ffn_up_w, L0_ffn_down_w,
           L0_ffn_gate_s, L0_ffn_up_s, L0_ffn_down_s):
    x = np.ascontiguousarray(np.asarray(L0_attn_x, np.float32)[0])
    Wq = np.asarray(L0_attn_Wq, np.float32)
    Wk = np.asarray(L0_attn_Wk, np.float32)
    Wv = np.asarray(L0_attn_Wv, np.float32)
    Wo = np.asarray(L0_attn_Wo, np.float32)
    gwf = np.asarray(L0_ffn_gate_w, np.float32)
    uwf = np.asarray(L0_ffn_up_w, np.float32)
    dwf = np.asarray(L0_ffn_down_w, np.float32)
    gsf = np.asarray(L0_ffn_gate_s, np.float32)
    usf = np.asarray(L0_ffn_up_s, np.float32)
    dsf = np.asarray(L0_ffn_down_s, np.float32)

    g_host = np.ascontiguousarray(
        gwf.reshape(IK, 128, HK, 128).transpose(3, 0, 2, 1))
    u_host = np.ascontiguousarray(
        uwf.reshape(IK, 128, HK, 128).transpose(3, 0, 2, 1))
    d_host = np.ascontiguousarray(
        dwf.reshape(4, 512, IK, 128).transpose(3, 2, 0, 1))
    gs_host = np.ascontiguousarray(gsf.reshape(IK, 128).T)
    us_host = np.ascontiguousarray(usf.reshape(IK, 128).T)
    ds_host = np.ascontiguousarray(dsf.reshape(HK, 128).T)

    nc = build()
    in_maps = []
    for r in range(R):
        xs = x[TL * r:TL * (r + 1)].T
        x_host = np.ascontiguousarray(
            xs.reshape(HK, 128, TL).transpose(1, 0, 2))
        wq_r = np.ascontiguousarray(
            Wq[:, 256 * r:256 * (r + 1)].reshape(HK, 128, 256)
            .transpose(1, 0, 2))
        wk_r = np.ascontiguousarray(
            Wk[:, 256 * r:256 * (r + 1)].reshape(HK, 128, 256)
            .transpose(1, 0, 2))
        wv_r = np.ascontiguousarray(
            Wv[:, 256 * r:256 * (r + 1)].reshape(HK, 128, 256)
            .transpose(1, 0, 2))
        wo_r = np.ascontiguousarray(
            Wo[256 * r:256 * (r + 1), :].reshape(2, 128, D)
            .transpose(1, 0, 2))
        in_maps.append({
            "xT": x_host, "wq": wq_r, "wk": wk_r, "wv": wv_r, "wo": wo_r,
            "gw": g_host, "uw": u_host, "dw": d_host,
            "gs": gs_host, "us": us_host, "ds": ds_host,
        })

    res = run_bass_kernel_spmd(nc, in_maps, core_ids=list(range(R)))

    y = np.empty((1, S, D), np.float32)
    for r in range(R):
        yT = np.asarray(res.results[r]["yT"])
        y[0, TL * r:TL * (r + 1), :] = (
            yT.transpose(2, 1, 0).reshape(TL, D))
    return y


# version 11
# speedup vs baseline: 1.0943x; 1.0034x over previous
from contextlib import ExitStack

import numpy as np

import concourse.bass as bass
import concourse.mybir as mybir
from concourse.bass_utils import run_bass_kernel_spmd

F32 = mybir.dt.float32
BF16 = mybir.dt.bfloat16
AF = mybir.ActivationFunctionType
ALU = mybir.AluOpType
AX = mybir.AxisListType

R = 8
D = 2048
S = 8192
TL = 1024
HK = D // 128
IK = 44
INTER = 5632
NHL = 2
SCALE = float(1.0 / np.sqrt(128.0))
EPS = 1e-5
NEG = -1.0e30
INV_D = float(1.0 / D)


class Ctr:

    def __init__(self, sem):
        self.sem = sem
        self.n = 0

    def tok(self):
        return (self, self.n)


class Prog:

    ENGINES = ("sync", "gpsimd", "tensor", "vector", "scalar")

    def __init__(self):
        self.items = {e: [] for e in self.ENGINES}

    def emit(self, eng, fn, waits=(), incs=()):
        rec = []
        for c, k in incs:
            c.n += k
            rec.append((c, k))
        self.items[eng].append((fn, [w for w in waits if w is not None], rec))

    def inc_last(self, eng, ctr, k=1):
        ctr.n += k
        self.items[eng][-1][2].append((ctr, k))

    def finalize(self, block):
        def run(eng_obj, eng_name):
            last_wait = {}
            for fn, waits, incs in self.items[eng_name]:
                for ctr, val in waits:
                    if val <= 0:
                        continue
                    if last_wait.get(id(ctr), -1) >= val:
                        continue
                    eng_obj.wait_ge(ctr.sem, val)
                    last_wait[id(ctr)] = val
                if fn is None:
                    continue
                instr = fn(eng_obj)
                for ctr, k in incs:
                    instr.then_inc(ctr.sem, k)

        if self.items["sync"]:
            block.sync(lambda e: run(e, "sync"))
        if self.items["gpsimd"]:
            block.gpsimd(lambda e: run(e, "gpsimd"))
        if self.items["tensor"]:
            block.tensor(lambda e: run(e, "tensor"))
        if self.items["vector"]:
            block.vector(lambda e: run(e, "vector"))
        if self.items["scalar"]:
            block.scalar(lambda e: run(e, "scalar"))


def build(dbg=False, phases=6, probe=None):
    nc = bass.Bass("TRN2", target_bir_lowering=False,
                   detect_race_conditions=False)
    KD = "ExternalOutput" if dbg else "Internal"

    xT = nc.dram_tensor("xT", [128, HK, TL], F32, kind="ExternalInput").ap()
    wq = nc.dram_tensor("wq", [128, HK, 256], F32, kind="ExternalInput").ap()
    wk = nc.dram_tensor("wk", [128, HK, 256], F32, kind="ExternalInput").ap()
    wv = nc.dram_tensor("wv", [128, HK, 256], F32, kind="ExternalInput").ap()
    wo = nc.dram_tensor("wo", [128, 2, D], F32, kind="ExternalInput").ap()
    gw = nc.dram_tensor("gw", [128, IK, HK, 128], F32, kind="ExternalInput").ap()
    uw = nc.dram_tensor("uw", [128, IK, HK, 128], F32, kind="ExternalInput").ap()
    dw = nc.dram_tensor("dw", [128, IK, 4, 512], F32, kind="ExternalInput").ap()
    gs = nc.dram_tensor("gs", [128, IK], F32, kind="ExternalInput").ap()
    us = nc.dram_tensor("us", [128, IK], F32, kind="ExternalInput").ap()
    ds = nc.dram_tensor("ds", [128, HK], F32, kind="ExternalInput").ap()

    yT = nc.dram_tensor("yT", [128, HK, TL], F32, kind="ExternalOutput").ap()

    x_dram = nc.dram_tensor("x_dram", [128, HK, TL], F32, kind=KD).ap()
    cc_xn_in = [nc.dram_tensor(f"cc_xn_in{g}", [128, HK * 512], F32,
                               kind=KD).ap() for g in range(2)]
    cc_xn_out = [nc.dram_tensor(f"cc_xn_out{g}", [R * 128, HK * 512],
                                F32).ap() for g in range(2)]
    qd = nc.dram_tensor("qd", [NHL, 128, S], F32, kind=KD).ap()
    kd = nc.dram_tensor("kd", [NHL, 128, S], F32, kind=KD).ap()
    vtm = nc.dram_tensor("vtm", [S, 256], F32, kind=KD).ap()
    attn_part = nc.dram_tensor("attn_part", [S, D], F32, kind=KD).ap()
    attn_own = nc.dram_tensor("attn_own", [TL, D], F32, kind=KD).ap()

    p = Prog()

    with ExitStack() as stk:
        e = stk.enter_context

        ones_col = e(nc.sbuf_tensor("ones_col", [1, 128], F32))
        ones128 = e(nc.sbuf_tensor("ones128", [128, 1], F32))
        ones_sq = e(nc.sbuf_tensor("ones_sq", [128, 128], F32))
        ident = e(nc.sbuf_tensor("ident", [128, 128], F32))
        masks = e(nc.sbuf_tensor("masks", [128, 4 * 512], F32))
        gs_sb = e(nc.sbuf_tensor("gs_sb", [128, IK], F32))
        us_sb = e(nc.sbuf_tensor("us_sb", [128, IK], F32))
        dsh_sb = e(nc.sbuf_tensor("dsh_sb", [128, HK], F32))
        maxQn = e(nc.sbuf_tensor("maxQn", [1, NHL * 16], F32))
        maxKn = e(nc.sbuf_tensor("maxKn", [1, NHL], F32))
        negCs = e(nc.sbuf_tensor("negCs", [1, NHL * 16], F32))
        rows = e(nc.sbuf_tensor("rows", [1, 6 * 512], F32))

        xb_sb = e(nc.sbuf_tensor("xb_sb", [128, 2 * 512], F32))
        sq_sb = e(nc.sbuf_tensor("sq_sb", [128, 512], F32))
        tmp_sb = e(nc.sbuf_tensor("tmp_sb", [128, 512], F32))
        out_sb = e(nc.sbuf_tensor("out_sb", [128, 2 * 512], F32))
        xr_sb = e(nc.sbuf_tensor("xr_sb", [128, 2 * 512], F32))
        negc_sb = e(nc.sbuf_tensor("negc_sb", [128, 1], F32))

        arena = e(nc.sbuf_tensor("arena", [128, 43008], F32))
        A = arena
        xnf_sb = A[:, 0:512]
        r1f_sb = A[:, 512:1024]
        sg_sb = A[:, 1536:2048]
        hdn_sb = A[:, 2048:24576]
        xnp_sb = A[:, 24576:36864].bitcast(BF16)
        gbf_sb = A[:, 36864:40960].bitcast(BF16)
        dbf_sb = A[:, 40960:41472].bitcast(BF16)
        hsp_sb = A[:, 41472:43008].bitcast(BF16)
        xnt_sb = A[:, 0:16384]
        wq_sb = A[:, 16384:20480]
        wk_sb = A[:, 20480:24576]
        wv_sb = A[:, 24576:28672]
        ev_sb = A[:, 28672:32768]
        q_sb = A[:, 0:1024]
        k_sb = A[:, 1024:3072]
        v_sb = A[:, 3072:5120]
        p_sb = A[:, 5120:7168]
        linv_sb = A[:, 7168:7680]
        oT_sb = A[:, 7680:8704]
        wo_sb = A[:, 8704:12800]

        B = [e(nc.psum_tensor(f"pb{i}", [128, 512], F32)) for i in range(8)]

        def C(name):
            return Ctr(e(nc.semaphore(name)))

        MM, VE, ACT, GP, CC = C("s_mm"), C("s_ve"), C("s_ac"), C("s_gp"), \
            C("s_cc")
        sc_ld = C("sc_ld")
        wqkv_ld = C("wqkv_ld")
        wo_ld = C("wo_ld")
        xb_ld = [C("xb_ld0"), C("xb_ld1")]
        wgu_ld = [C("wgu_ld0"), C("wgu_ld1")]
        wd_ld = [C("wd_ld0"), C("wd_ld1")]
        xr_ld = C("xr_ld")
        xnt_ld = [C("xnt_ld0"), C("xnt_ld1")]
        k_ld = [C(f"k_ld{i}") for i in range(4)]
        v_ld = [C(f"v_ld{i}") for i in range(4)]
        q_ld = C("q_ld")
        out_st = [C("out_st0"), C("out_st1"), C("out_st2"), C("out_st3")]
        ev_st = [C("ev_st0"), C("ev_st1"), C("ev_st2"), C("ev_st3")]
        vt_st = [C("vt_st0"), C("vt_st1")]

        block = e(nc.Block())

        def dma(dst, src):
            return lambda g: g.dma_start(out=dst, in_=src)

        def all_store_toks():
            return [s.tok() for s in out_st + ev_st + vt_st]

        p.emit("sync", dma(gs_sb[:], gs), incs=[(sc_ld, 16)])
        p.emit("sync", dma(us_sb[:], us), incs=[(sc_ld, 16)])
        p.emit("sync", dma(dsh_sb[:], ds), incs=[(sc_ld, 16)])
        t_scales = sc_ld.tok()

        p.emit("vector", lambda v: v.memset(ones_col[:], 1.0), incs=[(VE, 1)])
        p.emit("vector", lambda v: v.memset(ones128[:], 1.0), incs=[(VE, 1)])
        p.emit("vector", lambda v: v.memset(ones_sq[:], 1.0), incs=[(VE, 1)])
        p.emit("vector", lambda v: v.memset(maxKn[:], 0.0), incs=[(VE, 1)])
        p.emit("vector", lambda v: v.tensor_scalar(
            dsh_sb[:], dsh_sb[:], 0.5, None, ALU.mult),
            waits=[t_scales], incs=[(VE, 1)])
        ve_const = VE.n

        p.emit("gpsimd", lambda g: g.memset(ident[:], 0.0), incs=[(GP, 1)])
        p.emit("gpsimd", lambda g: g.affine_select(
            out=ident[:], in_=ident[:], compare_op=ALU.not_equal, fill=1.0,
            base=0, pattern=[[-1, 128]], channel_multiplier=1),
            waits=[(GP, GP.n)], incs=[(GP, 1)])
        for sft in range(4):
            p.emit("gpsimd", lambda g, sft=sft: g.memset(
                masks[:, sft * 512:(sft + 1) * 512], 0.0), incs=[(GP, 1)])
            p.emit("gpsimd", lambda g, sft=sft: g.affine_select(
                out=masks[:, sft * 512:(sft + 1) * 512],
                in_=masks[:, sft * 512:(sft + 1) * 512],
                compare_op=ALU.is_ge, fill=NEG,
                base=-128 * sft, pattern=[[1, 512]], channel_multiplier=-1),
                waits=[(GP, GP.n)], incs=[(GP, 1)])
        gp_const = GP.n

        bank_free = {}
        xfree = [None, None]
        sqfree = [None]
        outfree = [None] * 2
        xrfree = [None] * 2

        def ln_stats(src, c0, c1, extra_first_wait=None):
            for hc in range(HK):
                sl = hc % 2
                p.emit("sync", dma(xb_sb[:, sl * 512:(sl + 1) * 512],
                                   src[:, hc, c0:c1]),
                       waits=[xfree[sl]] +
                             ([extra_first_wait] if hc == 0 else []),
                       incs=[(xb_ld[sl], 16)])
                t_x = xb_ld[sl].tok()
                p.emit("tensor", lambda t, sl=sl, hc=hc: t.matmul(
                    B[0][:], ones_sq[:],
                    xb_sb[:, sl * 512:(sl + 1) * 512],
                    start=(hc == 0), stop=(hc == HK - 1)),
                    waits=[t_x, (VE, ve_const)] +
                          ([bank_free.get(0)] if hc == 0 else []))
                p.emit("vector", lambda v, sl=sl: v.tensor_mul(
                    sq_sb[:],
                    xb_sb[:, sl * 512:(sl + 1) * 512],
                    xb_sb[:, sl * 512:(sl + 1) * 512]),
                    waits=[t_x, sqfree[0]], incs=[(VE, 1)])
                ve_sq = VE.n
                p.emit("tensor", lambda t, hc=hc: t.matmul(
                    B[1][:], ones_sq[:],
                    sq_sb[:],
                    start=(hc == 0), stop=(hc == HK - 1)),
                    waits=[(VE, ve_sq)] +
                          ([bank_free.get(1)] if hc == 0 else []),
                    incs=[(MM, 1)])
                xfree[sl] = (MM, MM.n)
                sqfree[0] = (MM, MM.n)
            return MM.n

        def ln_rows(mm_stats):
            mrow = rows[:, 0:512]
            srow = rows[:, 512:1024]
            m2row = rows[:, 1024:1536]
            vrow = rows[:, 1536:2048]
            rsrow = rows[:, 2048:2560]
            mrsrow = rows[:, 2560:3072]
            p.emit("vector", lambda v: v.tensor_scalar(
                mrow, B[0][0:1, :], INV_D, None, ALU.mult),
                waits=[(MM, mm_stats)])
            p.emit("vector", lambda v: v.tensor_scalar(
                srow, B[1][0:1, :], INV_D, None, ALU.mult))
            p.emit("vector", lambda v: v.tensor_mul(m2row, mrow, mrow))
            p.emit("vector", lambda v: v.tensor_sub(vrow, srow, m2row))
            p.emit("vector", lambda v: v.tensor_single_scalar(
                vrow, vrow, EPS, ALU.add), incs=[(VE, 1)])
            p.emit("scalar", lambda s: s.activation(
                srow, vrow, AF.Sqrt), waits=[(VE, VE.n)], incs=[(ACT, 1)])
            p.emit("vector", lambda v: v.reciprocal(rsrow, srow),
                   waits=[(ACT, ACT.n)])
            p.emit("vector", lambda v: v.tensor_mul(mrsrow, mrow, rsrow),
                   incs=[(VE, 1)])
            ve_rows = VE.n
            p.emit("tensor", lambda t: t.matmul(
                B[2][:], ones_col[:], rsrow, start=True, stop=True),
                waits=[(VE, ve_rows), bank_free.get(2)])
            p.emit("tensor", lambda t: t.matmul(
                B[3][:], ones_col[:], mrsrow, start=True, stop=True),
                waits=[bank_free.get(3)], incs=[(MM, 1)])
            return MM.n

        def ffn_pass(src, dst, first_wait=None):
            hdn_ready = [None] * IK
            wfree = [None, None]
            dfree = [None, None]
            gb_free = [None, None]
            sg_free = [None]
            obfree = [None] * 4
            hspfree = [None, None]
            xnp_free = [None]
            xh = xnp_sb[:, 0:8192]
            xm = xnp_sb[:, 8192:16384]
            xl = xnp_sb[:, 16384:24576]
            for q5 in range(2):
                c0, c1 = q5 * 512, (q5 + 1) * 512
                mm_stats = ln_stats(src, c0, c1,
                                    first_wait if q5 == 0 else None)
                mm_bcast = ln_rows(mm_stats)
                for hc in range(HK):
                    sl = hc % 2
                    s5 = slice(hc * 512, (hc + 1) * 512)
                    p.emit("sync", dma(xb_sb[:, sl * 512:(sl + 1) * 512],
                                       src[:, hc, c0:c1]),
                           waits=[xfree[sl]], incs=[(xb_ld[sl], 16)])
                    t_x = xb_ld[sl].tok()
                    p.emit("vector", lambda v, sl=sl: v.tensor_mul(
                        tmp_sb[:], xb_sb[:, sl * 512:(sl + 1) * 512], B[2][:]),
                        waits=[t_x, (MM, mm_bcast)] +
                              ([xnp_free[0]] if hc == 0 else []))
                    p.emit("vector", lambda v: v.tensor_sub(
                        xnf_sb[:], tmp_sb[:], B[3][:]))
                    p.emit("vector", lambda v, s5=s5: v.tensor_copy(
                        xh[:, s5], xnf_sb[:]))
                    p.emit("vector", lambda v, s5=s5: v.tensor_sub(
                        xm[:, s5], xnf_sb[:], xh[:, s5]))
                    p.emit("vector", lambda v, s5=s5: v.tensor_sub(
                        r1f_sb[:], xnf_sb[:], xh[:, s5]))
                    p.emit("vector", lambda v, s5=s5: v.tensor_sub(
                        xl[:, s5], r1f_sb[:], xm[:, s5]), incs=[(VE, 1)])
                    xfree[sl] = (VE, VE.n)
                ve_xn = VE.n
                bank_free[2] = (VE, ve_xn)
                bank_free[3] = (VE, ve_xn)

                for isl in range(IK):
                    sl = isl % 2
                    gslab = gbf_sb[:, (2 * sl) * 2048:(2 * sl + 1) * 2048]
                    uslab = gbf_sb[:, (2 * sl + 1) * 2048:(2 * sl + 2) * 2048]
                    p.emit("gpsimd", dma(
                        gslab[:, 0:1024].rearrange("p (c n) -> p c n", n=128),
                        gw[:, isl, 0:8]), waits=[wfree[sl]],
                        incs=[(wgu_ld[sl], 16)])
                    p.emit("gpsimd", dma(
                        gslab[:, 1024:2048].rearrange("p (c n) -> p c n",
                                                      n=128),
                        gw[:, isl, 8:16]), incs=[(wgu_ld[sl], 16)])
                    p.emit("gpsimd", dma(
                        uslab[:, 0:1024].rearrange("p (c n) -> p c n", n=128),
                        uw[:, isl, 0:8]), incs=[(wgu_ld[sl], 16)])
                    p.emit("gpsimd", dma(
                        uslab[:, 1024:2048].rearrange("p (c n) -> p c n",
                                                      n=128),
                        uw[:, isl, 8:16]), incs=[(wgu_ld[sl], 16)])
                    t_w = wgu_ld[sl].tok()
                    gbank, ubank = B[2 * sl], B[2 * sl + 1]
                    for wslab, bank, gfree in ((gslab, gbank, gb_free[sl]),
                                               (uslab, ubank, None)):
                        first = True
                        for c in range(HK):
                            for xp in (xh, xm, xl):
                                p.emit("tensor", lambda t, wslab=wslab, c=c,
                                       bank=bank, xp=xp: t.matmul(
                                           bank[:],
                                           wslab[:, c * 128:(c + 1) * 128],
                                           xp[:, c * 512:(c + 1) * 512],
                                           start=(c == 0 and xp is xh),
                                           stop=(c == HK - 1 and xp is xl)),
                                       waits=([t_w, (VE, ve_xn), gfree]
                                              if first else []))
                                first = False
                        p.inc_last("tensor", MM, 1)
                        if wslab is gslab:
                            mm_gate = MM.n
                        else:
                            mm_up = MM.n
                    wfree[sl] = (MM, MM.n)
                    p.emit("scalar", lambda s, gbank=gbank, isl=isl:
                           s.activation(sg_sb[:], gbank[:], AF.Silu,
                                        scale=gs_sb[:, isl:isl + 1]),
                           waits=[(MM, mm_gate), sg_free[0], t_scales],
                           incs=[(ACT, 1)])
                    ac_silu = ACT.n
                    p.emit("vector", lambda v, ubank=ubank, isl=isl:
                           v.scalar_tensor_tensor(
                               hdn_sb[:, isl * 512:(isl + 1) * 512], ubank[:],
                               us_sb[:, isl:isl + 1], sg_sb[:],
                               ALU.mult, ALU.mult),
                           waits=[(MM, mm_up), (ACT, ac_silu)], incs=[(VE, 1)])
                    sg_free[0] = (VE, VE.n)
                    gb_free[sl] = (VE, VE.n)
                    hdn_ready[isl] = (VE, VE.n)
                xnp_free[0] = (MM, MM.n)

                for hg in range(4):
                    t_xr = [None, None]
                    for hc4 in range(2):
                        p.emit("sync", dma(
                            xr_sb[:, hc4 * 512:(hc4 + 1) * 512],
                            src[:, hg * 4 + hc4, c0:c1]),
                            waits=[xrfree[hc4]], incs=[(xr_ld, 16)])
                        xrfree[hc4] = None
                        t_xr[hc4] = xr_ld.tok()
                    for ic in range(IK):
                        sl = ic % 2
                        dslab = dbf_sb[:, sl * 512:(sl + 1) * 512]
                        p.emit("gpsimd", dma(dslab[:, 0:256],
                                             dw[:, ic, hg, 0:256]),
                               waits=[dfree[sl]], incs=[(wd_ld[sl], 16)])
                        p.emit("gpsimd", dma(dslab[:, 256:512],
                                             dw[:, ic, hg, 256:512]),
                               incs=[(wd_ld[sl], 16)])
                        t_d = wd_ld[sl].tok()
                        hs = hsp_sb[:, sl * 1536:(sl + 1) * 1536]
                        hh, hm, hl = hs[:, 0:512], hs[:, 512:1024], \
                            hs[:, 1024:1536]
                        hdn_ic = hdn_sb[:, ic * 512:(ic + 1) * 512]
                        p.emit("vector", lambda v, hh=hh, hdn_ic=hdn_ic:
                               v.tensor_copy(hh, hdn_ic),
                               waits=[hdn_ready[ic], hspfree[sl]])
                        p.emit("vector", lambda v, hm=hm, hdn_ic=hdn_ic,
                               hh=hh: v.tensor_sub(hm, hdn_ic, hh))
                        p.emit("vector", lambda v, hdn_ic=hdn_ic, hh=hh:
                               v.tensor_sub(r1f_sb[:], hdn_ic, hh))
                        p.emit("vector", lambda v, hl=hl, hm=hm:
                               v.tensor_sub(hl, r1f_sb[:], hm),
                               incs=[(VE, 1)])
                        ve_hsp = VE.n
                        first = True
                        for hc4 in range(4):
                            for pi_, hp in enumerate((hh, hm, hl)):
                                st_ = (ic == 0 and pi_ == 0)
                                sp_ = (ic == IK - 1 and pi_ == 2)
                                p.emit("tensor", lambda t, dslab=dslab,
                                       hc4=hc4, hp=hp, st_=st_, sp_=sp_:
                                       t.matmul(
                                           B[4 + hc4][:],
                                           dslab[:, hc4 * 128:(hc4 + 1) * 128],
                                           hp, start=st_, stop=sp_),
                                       waits=([t_d, (VE, ve_hsp)] +
                                              ([obfree[0], obfree[1],
                                                obfree[2], obfree[3]]
                                               if ic == 0 else [])
                                              if first else []))
                                first = False
                        p.inc_last("tensor", MM, 1)
                        dfree[sl] = (MM, MM.n)
                        hspfree[sl] = (MM, MM.n)
                    mm_down = MM.n
                    for hc4 in range(4):
                        hcg = hg * 4 + hc4
                        sl2 = hc4 % 2
                        if hc4 >= 2:
                            p.emit("sync", dma(
                                xr_sb[:, sl2 * 512:(sl2 + 1) * 512],
                                src[:, hcg, c0:c1]),
                                waits=[xrfree[sl2]], incs=[(xr_ld, 16)])
                            t_xr[sl2] = xr_ld.tok()
                        p.emit("vector", lambda v, sl2=sl2, hc4=hc4, hcg=hcg:
                               v.scalar_tensor_tensor(
                                   out_sb[:, sl2 * 512:(sl2 + 1) * 512],
                                   B[4 + hc4][:], dsh_sb[:, hcg:hcg + 1],
                                   xr_sb[:, sl2 * 512:(sl2 + 1) * 512],
                                   ALU.mult, ALU.add),
                               waits=[(MM, mm_down), t_xr[sl2],
                                      outfree[sl2]],
                               incs=[(VE, 1)])
                        obfree[hc4] = (VE, VE.n)
                        xrfree[sl2] = (VE, VE.n)
                        p.emit("sync", dma(
                            dst[:, hcg, c0:c1],
                            out_sb[:, sl2 * 512:(sl2 + 1) * 512]),
                            waits=[(VE, VE.n)], incs=[(out_st[sl2], 16)])
                        outfree[sl2] = out_st[sl2].tok()

        ffn_pass(xT, x_dram)
        barrier_p1 = all_store_toks()

        if phases >= 2:
            onfree = [None, None]
            for q5 in range(2):
                c0, c1 = q5 * 512, (q5 + 1) * 512
                mm_stats = ln_stats(x_dram, c0, c1,
                                    barrier_p1[0] if q5 == 0 else None)
                mm_bcast = ln_rows(mm_stats)
                for hc in range(HK):
                    sl = hc % 2
                    p.emit("sync", dma(xb_sb[:, sl * 512:(sl + 1) * 512],
                                       x_dram[:, hc, c0:c1]),
                           waits=[xfree[sl]], incs=[(xb_ld[sl], 16)])
                    t_x = xb_ld[sl].tok()
                    p.emit("vector", lambda v, sl=sl: v.tensor_mul(
                        tmp_sb[:], xb_sb[:, sl * 512:(sl + 1) * 512], B[2][:]),
                        waits=[t_x, (MM, mm_bcast), onfree[sl]])
                    p.emit("vector", lambda v, sl=sl: v.tensor_sub(
                        out_sb[:, sl * 512:(sl + 1) * 512], tmp_sb[:], B[3][:]),
                        incs=[(VE, 1)])
                    xfree[sl] = (VE, VE.n)
                    p.emit("sync", dma(
                        cc_xn_in[q5][:, hc * 512:(hc + 1) * 512],
                        out_sb[:, sl * 512:(sl + 1) * 512]),
                        waits=[(VE, VE.n)], incs=[(out_st[sl], 16)])
                    onfree[sl] = out_st[sl].tok()
                    outfree[sl] = out_st[sl].tok()
                bank_free[2] = (VE, VE.n)
                bank_free[3] = (VE, VE.n)
                p.emit("gpsimd", lambda g, q5=q5: g.collective_compute(
                    "AllGather", ALU.bypass,
                    replica_groups=[list(range(R))],
                    ins=[cc_xn_in[q5][:]], outs=[cc_xn_out[q5][:]]),
                    waits=[out_st[0].tok(), out_st[1].tok()], incs=[(CC, 1)])
            cc_xn_done = CC.n


        if phases >= 3:
            p.emit("sync", dma(wq_sb.rearrange("p (c n) -> p c n", n=256), wq),
                   waits=[(CC, cc_xn_done - 1)], incs=[(wqkv_ld, 16)])
            p.emit("sync", dma(wk_sb.rearrange("p (c n) -> p c n", n=256), wk),
                   incs=[(wqkv_ld, 16)])
            p.emit("sync", dma(wv_sb.rearrange("p (c n) -> p c n", n=256), wv),
                   incs=[(wqkv_ld, 16)])
            t_wqkv = wqkv_ld.tok()
            xn_view = [cco.rearrange("(r p) (c t) -> r p c t", p=128, t=512)
                       for cco in cc_xn_out]
            xnt_free = [None, None]
            ev_free = [None] * 8
            b7_free = None
            for ni, nt in enumerate(list(range(0, 16, 2)) +
                                    list(range(1, 16, 2))):
                blk = nt // 2
                q5 = nt % 2
                sl = ni % 2
                xnt = xnt_sb[:, sl * HK * 512:(sl + 1) * HK * 512]
                p.emit("sync", dma(
                    xnt.rearrange("p (c t) -> p c t", t=512),
                    xn_view[q5][blk, :, :, :]),
                    waits=[xnt_free[sl], (CC, cc_xn_done - 1 + q5)],
                    incs=[(xnt_ld[sl], 16)])
                t_xnt = xnt_ld[sl].tok()
                for pi, wsb in ((0, wq_sb), (1, wk_sb), (2, wv_sb)):
                    for dc in range(2):
                        bank = B[pi * 2 + dc]
                        for c in range(HK):
                            p.emit("tensor", lambda t, wsb=wsb, c=c, dc=dc,
                                   bank=bank, xnt=xnt: t.matmul(
                                       bank[:],
                                       wsb[:, c * 256 + dc * 128:
                                           c * 256 + dc * 128 + 128],
                                       xnt[:, c * 512:(c + 1) * 512],
                                       start=(c == 0), stop=(c == HK - 1)),
                                   waits=([t_xnt, t_wqkv,
                                           bank_free.get(pi * 2 + dc)]
                                          if c == 0 else []))
                        p.inc_last("tensor", MM, 1)
                        bank_free[pi * 2 + dc] = None
                mm_qkv = MM.n
                xnt_free[sl] = (MM, MM.n)
                for pi, dest in ((0, qd), (1, kd)):
                    for dc in range(2):
                        bank = B[pi * 2 + dc]
                        mm_bank = mm_qkv - 5 + (pi * 2 + dc)
                        esl = pi * 2 + dc
                        ev = ev_sb[:, esl * 512:(esl + 1) * 512]
                        p.emit("vector", lambda v, ev=ev, bank=bank: v.tensor_copy(
                            ev, bank[:]),
                            waits=[(MM, mm_bank), ev_free[esl]], incs=[(VE, 1)])
                        bank_free[pi * 2 + dc] = (VE, VE.n)
                        p.emit("sync", dma(dest[dc, :, nt * 512:nt * 512 + 512],
                                           ev),
                               waits=[(VE, VE.n)], incs=[(ev_st[esl], 16)])
                        st_ev = ev_st[esl].tok()
                        p.emit("vector", lambda v, ev=ev: v.tensor_mul(
                            tmp_sb[:], ev, ev), waits=[b7_free], incs=[(VE, 1)])
                        p.emit("tensor", lambda t: t.matmul(
                            B[7][:], ones_sq[:], tmp_sb[:],
                            start=True, stop=True),
                            waits=[(VE, VE.n)], incs=[(MM, 1)])
                        p.emit("vector", lambda v: v.reduce_max(
                            rows[:, 0:1], B[7][0:1, :], axis=AX.X),
                            waits=[(MM, MM.n)], incs=[(VE, 1)])
                        b7_free = (VE, VE.n)
                        t_red = (VE, VE.n)
                        if pi == 0:
                            p.emit("vector", lambda v, dc=dc, nt=nt: v.tensor_copy(
                                maxQn[0:1, dc * 16 + nt:dc * 16 + nt + 1],
                                rows[:, 0:1]), waits=[t_red], incs=[(VE, 1)])
                        else:
                            p.emit("vector", lambda v, dc=dc: v.tensor_tensor(
                                maxKn[0:1, dc:dc + 1], maxKn[0:1, dc:dc + 1],
                                rows[:, 0:1], ALU.max), waits=[t_red],
                                incs=[(VE, 1)])
                        ev_free[esl] = st_ev
                for dc in range(2):
                    bank = B[4 + dc]
                    mm_bank = mm_qkv - 1 + dc
                    esl = 4 + dc
                    ev = ev_sb[:, esl * 512:(esl + 1) * 512]
                    p.emit("vector", lambda v, ev=ev, bank=bank: v.tensor_copy(
                        ev, bank[:]), waits=[(MM, mm_bank), ev_free[esl]],
                        incs=[(VE, 1)])
                    ve_ev = VE.n
                    bank_free[4 + dc] = (VE, VE.n)
                    for tt in range(4):
                        p.emit("tensor", lambda t, ev=ev, tt=tt: t.transpose(
                            B[6][:, tt * 128:(tt + 1) * 128],
                            ev[:, tt * 128:(tt + 1) * 128], ident[:]),
                            waits=([(VE, ve_ev), (GP, gp_const),
                                    bank_free.get(6)] if tt == 0 else []))
                    p.inc_last("tensor", MM, 1)
                    ev_free[esl] = (MM, MM.n)
                    esl2 = 6 + dc
                    ev2 = ev_sb[:, esl2 * 512:(esl2 + 1) * 512]
                    p.emit("vector", lambda v, ev2=ev2: v.tensor_copy(
                        ev2, B[6][:]), waits=[(MM, MM.n), ev_free[esl2]],
                        incs=[(VE, 1)])
                    bank_free[6] = (VE, VE.n)
                    p.emit("sync", dma(
                        vtm[nt * 512:(nt + 1) * 512,
                            dc * 128:(dc + 1) * 128].rearrange(
                                "(tt p) d -> p tt d", p=128),
                        ev2.rearrange("p (tt d) -> p tt d", d=128)),
                        waits=[(VE, VE.n)], incs=[(vt_st[dc], 16)])
                    ev_free[esl2] = vt_st[dc].tok()
            barrier_qkv = [s.tok() for s in ev_st + vt_st]

            for dc in range(NHL):
                p.emit("vector", lambda v, dc=dc: v.tensor_scalar(
                    negCs[0:1, dc * 16:(dc + 1) * 16],
                    maxQn[0:1, dc * 16:(dc + 1) * 16],
                    maxKn[0:1, dc:dc + 1], None, ALU.mult), incs=[(VE, 1)])
            p.emit("scalar", lambda s: s.activation(negCs[:], negCs[:], AF.Sqrt),
                   waits=[(VE, VE.n)], incs=[(ACT, 1)])
            p.emit("vector", lambda v: v.tensor_scalar(
                negCs[:], negCs[:], -SCALE, None, ALU.mult),
                waits=[(ACT, ACT.n)], incs=[(VE, 1)])
            ve_ctab = VE.n


        if phases >= 4:
            p.emit("sync", dma(wo_sb.rearrange("p (c n) -> p c n", n=D), wo),
                   waits=barrier_qkv, incs=[(wo_ld, 16)])
            t_wo = wo_ld.tok()
            kfree = [None] * 4
            vfree = [None] * 4
            pfree = [None] * 4
            qfree = [None, None]
            l_free = None
            o_free = None
            negc_free = None
            linv_free = None
            oT_free = [None, None]
            wo_bfree = [None, None]
            wo_outfree = [outfree[0], outfree[1]]
            sbank_free = [None, None]
            for qc in range(16):
                for h in range(NHL):
                    qsl = (qc * NHL + h) % 2
                    qv = q_sb[:, qsl * 512:(qsl + 1) * 512]
                    p.emit("sync", dma(qv, qd[h, :, qc * 512:qc * 512 + 512]),
                           waits=[qfree[qsl]] +
                                 (barrier_qkv if qc == 0 and h == 0 else []),
                           incs=[(q_ld, 16)])
                    t_q = q_ld.tok()
                    p.emit("tensor", lambda t, h=h, qc=qc: t.matmul(
                        B[4][:, 0:1], ones_col[:],
                        negCs[0:1, h * 16 + qc:h * 16 + qc + 1],
                        start=True, stop=True),
                        waits=[(VE, ve_ctab), negc_free], incs=[(MM, 1)])
                    p.emit("vector", lambda v: v.tensor_copy(
                        negc_sb[:], B[4][:, 0:1]),
                        waits=[(MM, MM.n)], incs=[(VE, 1)])
                    ve_negc = VE.n
                    for j in range(qc + 1):
                        ksl = j % 4
                        vsl = j % 4
                        p.emit("sync", dma(k_sb[:, ksl * 512:(ksl + 1) * 512],
                                           kd[h, :, j * 512:j * 512 + 512]),
                               waits=[kfree[ksl]], incs=[(k_ld[ksl], 16)])
                        t_k = k_ld[ksl].tok()
                        p.emit("sync", dma(
                            v_sb[:, vsl * 512:(vsl + 1) * 512].rearrange(
                                "p (tt d) -> p tt d", d=128),
                            vtm[j * 512:(j + 1) * 512,
                                h * 128:(h + 1) * 128].rearrange(
                                    "(tt p) d -> p tt d", p=128)),
                            waits=[vfree[vsl]], incs=[(v_ld[vsl], 16)])
                        t_v = v_ld[vsl].tok()
                        diag = (j == qc)
                        first = (j == 0)
                        last = (j == qc)
                        ac_exps = [None] * 4
                        for sft in range(4):
                            sb = sft % 2
                            p.emit("tensor", lambda t, sb=sb, ksl=ksl, sft=sft,
                                   qv=qv: t.matmul(
                                       B[sb][:],
                                       k_sb[:, ksl * 512 + sft * 128:
                                            ksl * 512 + sft * 128 + 128],
                                       qv, start=True, stop=True),
                                   waits=[t_k, t_q, sbank_free[sb]],
                                   incs=[(MM, 1)])
                            mm_s = MM.n
                            if diag:
                                p.emit("vector", lambda v, sb=sb, sft=sft:
                                       v.tensor_add(B[sb][:], B[sb][:],
                                                    masks[:, sft * 512:
                                                          (sft + 1) * 512]),
                                       waits=[(MM, mm_s), (GP, gp_const)],
                                       incs=[(VE, 1)])
                                dep = (VE, VE.n)
                            else:
                                dep = (MM, mm_s)
                            p.emit("scalar", lambda s, sb=sb, sft=sft:
                                   s.activation(
                                       p_sb[:, sft * 512:(sft + 1) * 512],
                                       B[sb][:], AF.Exp, bias=negc_sb[:],
                                       scale=SCALE),
                                   waits=[dep, pfree[sft], (VE, ve_negc)],
                                   incs=[(ACT, 1)])
                            ac_exps[sft] = (ACT, ACT.n)
                            sbank_free[sb] = (ACT, ACT.n)
                        for sft in range(4):
                            p.emit("tensor", lambda t, sft=sft, first=first,
                                   last=last: t.matmul(
                                       B[2][:], ones_sq[:],
                                       p_sb[:, sft * 512:(sft + 1) * 512],
                                       start=(first and sft == 0),
                                       stop=(last and sft == 3)),
                                   waits=[ac_exps[sft]] +
                                         ([l_free] if first and sft == 0 else []))
                            p.emit("tensor", lambda t, sft=sft, vsl=vsl,
                                   first=first, last=last: t.matmul(
                                       B[3][:], v_sb[:, vsl * 512 + sft * 128:
                                                     vsl * 512 + sft * 128 + 128],
                                       p_sb[:, sft * 512:(sft + 1) * 512],
                                       start=(first and sft == 0),
                                       stop=(last and sft == 3)),
                                   waits=[t_v] +
                                         ([o_free] if first and sft == 0 else []),
                                   incs=[(MM, 1)])
                            pfree[sft] = (MM, MM.n)
                        kfree[ksl] = (MM, MM.n)
                        vfree[vsl] = (MM, MM.n)
                    qfree[qsl] = (MM, MM.n)
                    mm_attn = MM.n
                    negc_free = (ACT, ACT.n)
                    p.emit("vector", lambda v: v.reciprocal(
                        linv_sb[:], B[2][:]),
                        waits=[(MM, mm_attn)], incs=[(VE, 1)])
                    l_free = (VE, VE.n)
                    p.emit("vector", lambda v, h=h: v.tensor_mul(
                        oT_sb[:, h * 512:(h + 1) * 512], B[3][:], linv_sb[:]),
                        waits=[oT_free[h]], incs=[(VE, 1)])
                    o_free = (VE, VE.n)
                ve_oT = VE.n
                for tt in range(4):
                    for db in range(4):
                        wsl = (tt * 4 + db) % 2
                        bank = B[6 + wsl]
                        for dc in range(2):
                            p.emit("tensor", lambda t, tt=tt, db=db, dc=dc,
                                   bank=bank: t.matmul(
                                       bank[:],
                                       oT_sb[:, dc * 512 + tt * 128:
                                             dc * 512 + tt * 128 + 128],
                                       wo_sb[:, dc * D + db * 512:
                                             dc * D + db * 512 + 512],
                                       start=(dc == 0), stop=(dc == 1)),
                                   waits=([(VE, ve_oT), t_wo, wo_bfree[wsl]]
                                          if dc == 0 else []))
                        p.inc_last("tensor", MM, 1)
                        mm_wo = MM.n
                        p.emit("vector", lambda v, wsl=wsl, bank=bank:
                               v.tensor_copy(out_sb[:, wsl * 512:(wsl + 1) * 512],
                                             bank[:]),
                               waits=[(MM, mm_wo), wo_outfree[wsl]],
                               incs=[(VE, 1)])
                        wo_bfree[wsl] = (VE, VE.n)
                        p.emit("sync", dma(
                            attn_part[qc * 512 + tt * 128:
                                      qc * 512 + tt * 128 + 128,
                                      db * 512:(db + 1) * 512],
                            out_sb[:, wsl * 512:(wsl + 1) * 512]),
                            waits=[(VE, VE.n)], incs=[(out_st[wsl], 16)])
                        wo_outfree[wsl] = out_st[wsl].tok()
                oT_free[0] = (MM, MM.n)
                oT_free[1] = (MM, MM.n)
            outfree[0] = wo_outfree[0]
            outfree[1] = wo_outfree[1]


        if phases >= 5:
            p.emit("gpsimd", lambda g: g.collective_compute(
                "ReduceScatter", ALU.add, replica_groups=[list(range(R))],
                ins=[attn_part[:]], outs=[attn_own[:]]),
                waits=all_store_toks(), incs=[(CC, 1)])
            cc_rs_done = CC.n

            at_free = None
            xs_free = None
            x2_free = outfree[0]
            for t8 in range(8):
                for hg in range(4):
                    p.emit("sync", dma(
                        xb_sb[:, 0:512],
                        attn_own[t8 * 128:(t8 + 1) * 128,
                                 hg * 512:(hg + 1) * 512]),
                        waits=[(CC, cc_rs_done), at_free, xfree[0]],
                        incs=[(xb_ld[0], 16)])
                    xfree[0] = None
                    t_at = xb_ld[0].tok()
                    p.emit("sync", dma(
                        xr_sb[:, 0:512].rearrange("p (hc t) -> p hc t", t=128),
                        x_dram[:, hg * 4:(hg + 1) * 4,
                               t8 * 128:(t8 + 1) * 128]),
                        waits=[xs_free, xrfree[0]], incs=[(xr_ld, 16)])
                    xrfree[0] = None
                    t_xs = xr_ld.tok()
                    for hc4 in range(4):
                        p.emit("tensor", lambda t, hc4=hc4: t.transpose(
                            B[0][:, hc4 * 128:(hc4 + 1) * 128],
                            xb_sb[:, hc4 * 128:(hc4 + 1) * 128], ident[:]),
                            waits=([t_at, bank_free.get(0)]
                                   if hc4 == 0 else []))
                    p.inc_last("tensor", MM, 1)
                    at_free = (MM, MM.n)
                    p.emit("vector", lambda v: v.tensor_add(
                        out_sb[:, 0:512], B[0][:], xr_sb[:, 0:512]),
                        waits=[(MM, MM.n), t_xs, x2_free], incs=[(VE, 1)])
                    bank_free[0] = (VE, VE.n)
                    xs_free = (VE, VE.n)
                    p.emit("sync", dma(
                        x_dram[:, hg * 4:(hg + 1) * 4,
                               t8 * 128:(t8 + 1) * 128],
                        out_sb[:, 0:512].rearrange("p (hc t) -> p hc t", t=128)),
                        waits=[(VE, VE.n)], incs=[(out_st[0], 16)])
                    x2_free = out_st[0].tok()
            outfree[0] = x2_free


        if phases >= 6:
            bank_free.clear()
            ffn_pass(x_dram, yT, first_wait=x2_free)


        if probe == "tabs":
            p.emit("sync", dma(yT[0:1, 0, 0:32], negCs[:]),
                   waits=all_store_toks() + [(CC, CC.n), (MM, MM.n),
                                             (VE, VE.n), (ACT, ACT.n)],
                   incs=[(out_st[3], 16)])
            p.emit("sync", dma(yT[0:1, 0, 32:64], maxQn[:]),
                   incs=[(out_st[3], 16)])
            p.emit("sync", dma(yT[0:1, 0, 64:66], maxKn[:]),
                   incs=[(out_st[3], 16)])
        elif probe is not None:
            yflat = yT.rearrange("p c t -> p (c t)")
            srcs = {
                "x_dram": x_dram.rearrange("p c t -> p (c t)"),
                "cc_xn_in": cc_xn_in[0][:],
                "qd": qd.rearrange("h p s -> p h s"),
                "kd": kd.rearrange("h p s -> p h s"),
                "vtm": vtm.rearrange("(a p) d -> p a d", p=128),
                "part0": attn_part[0:1024].rearrange("(a p) d -> p a d", p=128),
                "attn_own": attn_own.rearrange("(a p) d -> p a d", p=128),
            }
            sp = srcs[probe]
            yv = yflat if len(sp.shape) == 2 else yflat.rearrange(
                "p (a d) -> p a d", d=sp.shape[-1])
            p.emit("sync", dma(yv, sp),
                   waits=all_store_toks() + [(CC, CC.n), (MM, MM.n),
                                             (VE, VE.n), (ACT, ACT.n)],
                   incs=[(out_st[3], 16)])

        p.emit("sync", None,
               waits=all_store_toks() + [(MM, MM.n), (VE, VE.n),
                                         (ACT, ACT.n), (CC, CC.n),
                                         (GP, GP.n)])

        p.finalize(block)

    return nc


def kernel(L0_attn_x, L0_attn_Wq, L0_attn_Wk, L0_attn_Wv, L0_attn_Wo,
           L0_ffn_gate_w, L0_ffn_up_w, L0_ffn_down_w,
           L0_ffn_gate_s, L0_ffn_up_s, L0_ffn_down_s):
    x = np.ascontiguousarray(np.asarray(L0_attn_x, np.float32)[0])
    Wq = np.asarray(L0_attn_Wq, np.float32)
    Wk = np.asarray(L0_attn_Wk, np.float32)
    Wv = np.asarray(L0_attn_Wv, np.float32)
    Wo = np.asarray(L0_attn_Wo, np.float32)
    gwf = np.asarray(L0_ffn_gate_w, np.float32)
    uwf = np.asarray(L0_ffn_up_w, np.float32)
    dwf = np.asarray(L0_ffn_down_w, np.float32)
    gsf = np.asarray(L0_ffn_gate_s, np.float32)
    usf = np.asarray(L0_ffn_up_s, np.float32)
    dsf = np.asarray(L0_ffn_down_s, np.float32)

    g_host = np.ascontiguousarray(
        gwf.reshape(IK, 128, HK, 128).transpose(3, 0, 2, 1))
    u_host = np.ascontiguousarray(
        uwf.reshape(IK, 128, HK, 128).transpose(3, 0, 2, 1))
    d_host = np.ascontiguousarray(
        dwf.reshape(4, 512, IK, 128).transpose(3, 2, 0, 1))
    gs_host = np.ascontiguousarray(gsf.reshape(IK, 128).T)
    us_host = np.ascontiguousarray(usf.reshape(IK, 128).T)
    ds_host = np.ascontiguousarray(dsf.reshape(HK, 128).T)

    nc = build()
    in_maps = []
    for r in range(R):
        xs = x[TL * r:TL * (r + 1)].T
        x_host = np.ascontiguousarray(
            xs.reshape(HK, 128, TL).transpose(1, 0, 2))
        wq_r = np.ascontiguousarray(
            Wq[:, 256 * r:256 * (r + 1)].reshape(HK, 128, 256)
            .transpose(1, 0, 2))
        wk_r = np.ascontiguousarray(
            Wk[:, 256 * r:256 * (r + 1)].reshape(HK, 128, 256)
            .transpose(1, 0, 2))
        wv_r = np.ascontiguousarray(
            Wv[:, 256 * r:256 * (r + 1)].reshape(HK, 128, 256)
            .transpose(1, 0, 2))
        wo_r = np.ascontiguousarray(
            Wo[256 * r:256 * (r + 1), :].reshape(2, 128, D)
            .transpose(1, 0, 2))
        in_maps.append({
            "xT": x_host, "wq": wq_r, "wk": wk_r, "wv": wv_r, "wo": wo_r,
            "gw": g_host, "uw": u_host, "dw": d_host,
            "gs": gs_host, "us": us_host, "ds": ds_host,
        })

    res = run_bass_kernel_spmd(nc, in_maps, core_ids=list(range(R)))

    y = np.empty((1, S, D), np.float32)
    for r in range(R):
        yT = np.asarray(res.results[r]["yT"])
        y[0, TL * r:TL * (r + 1), :] = (
            yT.transpose(2, 1, 0).reshape(TL, D))
    return y


# version 12 (final)
# speedup vs baseline: 1.1105x; 1.0148x over previous
from contextlib import ExitStack

import numpy as np

import concourse.bass as bass
import concourse.mybir as mybir
from concourse.bass_utils import run_bass_kernel_spmd

F32 = mybir.dt.float32
BF16 = mybir.dt.bfloat16
AF = mybir.ActivationFunctionType
ALU = mybir.AluOpType
AX = mybir.AxisListType

R = 8
D = 2048
S = 8192
TL = 1024
HK = D // 128
IK = 44
INTER = 5632
NHL = 2
SCALE = float(1.0 / np.sqrt(128.0))
EPS = 1e-5
NEG = -1.0e30
INV_D = float(1.0 / D)


class Ctr:

    def __init__(self, sem):
        self.sem = sem
        self.n = 0

    def tok(self):
        return (self, self.n)


class Prog:

    ENGINES = ("sync", "gpsimd", "tensor", "vector", "scalar")

    def __init__(self):
        self.items = {e: [] for e in self.ENGINES}

    def emit(self, eng, fn, waits=(), incs=()):
        rec = []
        for c, k in incs:
            c.n += k
            rec.append((c, k))
        self.items[eng].append((fn, [w for w in waits if w is not None], rec))

    def inc_last(self, eng, ctr, k=1):
        ctr.n += k
        self.items[eng][-1][2].append((ctr, k))

    def finalize(self, block):
        def run(eng_obj, eng_name):
            last_wait = {}
            for fn, waits, incs in self.items[eng_name]:
                for ctr, val in waits:
                    if val <= 0:
                        continue
                    if last_wait.get(id(ctr), -1) >= val:
                        continue
                    eng_obj.wait_ge(ctr.sem, val)
                    last_wait[id(ctr)] = val
                if fn is None:
                    continue
                instr = fn(eng_obj)
                for ctr, k in incs:
                    instr.then_inc(ctr.sem, k)

        if self.items["sync"]:
            block.sync(lambda e: run(e, "sync"))
        if self.items["gpsimd"]:
            block.gpsimd(lambda e: run(e, "gpsimd"))
        if self.items["tensor"]:
            block.tensor(lambda e: run(e, "tensor"))
        if self.items["vector"]:
            block.vector(lambda e: run(e, "vector"))
        if self.items["scalar"]:
            block.scalar(lambda e: run(e, "scalar"))


def build(dbg=False, phases=6, probe=None):
    nc = bass.Bass("TRN2", target_bir_lowering=False,
                   detect_race_conditions=False)
    KD = "ExternalOutput" if dbg else "Internal"

    xT = nc.dram_tensor("xT", [128, HK, TL], F32, kind="ExternalInput").ap()
    wq = nc.dram_tensor("wq", [128, HK, 256], F32, kind="ExternalInput").ap()
    wk = nc.dram_tensor("wk", [128, HK, 256], F32, kind="ExternalInput").ap()
    wv = nc.dram_tensor("wv", [128, HK, 256], F32, kind="ExternalInput").ap()
    wo = nc.dram_tensor("wo", [128, 2, D], F32, kind="ExternalInput").ap()
    gw = nc.dram_tensor("gw", [128, IK, HK, 128], F32, kind="ExternalInput").ap()
    uw = nc.dram_tensor("uw", [128, IK, HK, 128], F32, kind="ExternalInput").ap()
    dw = nc.dram_tensor("dw", [128, IK, 4, 512], F32, kind="ExternalInput").ap()
    gs = nc.dram_tensor("gs", [128, IK], F32, kind="ExternalInput").ap()
    us = nc.dram_tensor("us", [128, IK], F32, kind="ExternalInput").ap()
    ds = nc.dram_tensor("ds", [128, HK], F32, kind="ExternalInput").ap()

    yT = nc.dram_tensor("yT", [128, HK, TL], F32, kind="ExternalOutput").ap()

    x_dram = nc.dram_tensor("x_dram", [128, HK, TL], F32, kind=KD).ap()
    cc_xn_in = [nc.dram_tensor(f"cc_xn_in{g}", [128, HK * 512], F32,
                               kind=KD).ap() for g in range(2)]
    cc_xn_out = [nc.dram_tensor(f"cc_xn_out{g}", [R * 128, HK * 512],
                                F32).ap() for g in range(2)]
    qd = nc.dram_tensor("qd", [NHL, 128, S], F32, kind=KD).ap()
    kd = nc.dram_tensor("kd", [NHL, 128, S], F32, kind=KD).ap()
    vtm = nc.dram_tensor("vtm", [S, 256], F32, kind=KD).ap()
    attn_part = nc.dram_tensor("attn_part", [S, D], F32, kind=KD).ap()
    attn_own = nc.dram_tensor("attn_own", [TL, D], F32, kind=KD).ap()

    p = Prog()

    with ExitStack() as stk:
        e = stk.enter_context

        ones_col = e(nc.sbuf_tensor("ones_col", [1, 128], F32))
        ones128 = e(nc.sbuf_tensor("ones128", [128, 1], F32))
        ones_sq = e(nc.sbuf_tensor("ones_sq", [128, 128], F32))
        ident = e(nc.sbuf_tensor("ident", [128, 128], F32))
        gs_sb = e(nc.sbuf_tensor("gs_sb", [128, IK], F32))
        us_sb = e(nc.sbuf_tensor("us_sb", [128, IK], F32))
        dsh_sb = e(nc.sbuf_tensor("dsh_sb", [128, HK], F32))
        maxQn = e(nc.sbuf_tensor("maxQn", [1, NHL * 16], F32))
        maxKn = e(nc.sbuf_tensor("maxKn", [1, NHL], F32))
        negCs = e(nc.sbuf_tensor("negCs", [1, NHL * 16], F32))
        rows = e(nc.sbuf_tensor("rows", [1, 6 * 512], F32))

        xb_sb = e(nc.sbuf_tensor("xb_sb", [128, 2 * 512], F32))
        sq_sb = e(nc.sbuf_tensor("sq_sb", [128, 512], F32))
        tmp_sb = e(nc.sbuf_tensor("tmp_sb", [128, 512], F32))
        out_sb = e(nc.sbuf_tensor("out_sb", [128, 2 * 512], F32))
        xr_sb = e(nc.sbuf_tensor("xr_sb", [128, 2 * 512], F32))
        negc_sb = e(nc.sbuf_tensor("negc_sb", [128, 1], F32))

        arena = e(nc.sbuf_tensor("arena", [128, 44544], F32))
        A = arena
        xnf_sb = A[:, 0:512]
        r1f_sb = A[:, 512:1024]
        sg_sb = A[:, 1536:2048]
        hdn_sb = A[:, 2048:24576]
        xnp_sb = A[:, 24576:36864].bitcast(BF16)
        gbf_sb = A[:, 36864:40960].bitcast(BF16)
        dbf_sb = A[:, 40960:41472].bitcast(BF16)
        hsp_sb = A[:, 41472:44544].bitcast(BF16)
        masks = A[:, 32768:34816]
        xnt_sb = A[:, 0:16384]
        wq_sb = A[:, 16384:20480]
        wk_sb = A[:, 20480:24576]
        wv_sb = A[:, 24576:28672]
        ev_sb = A[:, 28672:32768]
        q_sb = A[:, 0:1024]
        k_sb = A[:, 1024:3072]
        v_sb = A[:, 3072:5120]
        p_sb = A[:, 5120:7168]
        linv_sb = A[:, 7168:7680]
        oT_sb = A[:, 7680:8704]
        wo_sb = A[:, 8704:12800]

        B = [e(nc.psum_tensor(f"pb{i}", [128, 512], F32)) for i in range(8)]

        def C(name):
            return Ctr(e(nc.semaphore(name)))

        MM, VE, ACT, GP, CC = C("s_mm"), C("s_ve"), C("s_ac"), C("s_gp"), \
            C("s_cc")
        sc_ld = C("sc_ld")
        wqkv_ld = C("wqkv_ld")
        wo_ld = C("wo_ld")
        xb_ld = [C("xb_ld0"), C("xb_ld1")]
        wgu_ld = [C("wgu_ld0"), C("wgu_ld1")]
        wd_ld = [C("wd_ld0"), C("wd_ld1")]
        xr_ld = C("xr_ld")
        xnt_ld = [C("xnt_ld0"), C("xnt_ld1")]
        k_ld = [C(f"k_ld{i}") for i in range(4)]
        v_ld = [C(f"v_ld{i}") for i in range(4)]
        q_ld = C("q_ld")
        out_st = [C("out_st0"), C("out_st1"), C("out_st2"), C("out_st3")]
        ev_st = [C("ev_st0"), C("ev_st1"), C("ev_st2"), C("ev_st3")]
        vt_st = [C("vt_st0"), C("vt_st1")]

        block = e(nc.Block())

        def dma(dst, src):
            return lambda g: g.dma_start(out=dst, in_=src)

        def all_store_toks():
            return [s.tok() for s in out_st + ev_st + vt_st]

        p.emit("sync", dma(gs_sb[:], gs), incs=[(sc_ld, 16)])
        p.emit("sync", dma(us_sb[:], us), incs=[(sc_ld, 16)])
        p.emit("sync", dma(dsh_sb[:], ds), incs=[(sc_ld, 16)])
        t_scales = sc_ld.tok()

        p.emit("vector", lambda v: v.memset(ones_col[:], 1.0), incs=[(VE, 1)])
        p.emit("vector", lambda v: v.memset(ones128[:], 1.0), incs=[(VE, 1)])
        p.emit("vector", lambda v: v.memset(ones_sq[:], 1.0), incs=[(VE, 1)])
        p.emit("vector", lambda v: v.memset(maxKn[:], 0.0), incs=[(VE, 1)])
        p.emit("vector", lambda v: v.tensor_scalar(
            dsh_sb[:], dsh_sb[:], 0.5, None, ALU.mult),
            waits=[t_scales], incs=[(VE, 1)])
        ve_const = VE.n

        p.emit("gpsimd", lambda g: g.memset(ident[:], 0.0), incs=[(GP, 1)])
        p.emit("gpsimd", lambda g: g.affine_select(
            out=ident[:], in_=ident[:], compare_op=ALU.not_equal, fill=1.0,
            base=0, pattern=[[-1, 128]], channel_multiplier=1),
            waits=[(GP, GP.n)], incs=[(GP, 1)])
        gp_const = GP.n

        def build_masks():
            for sft in range(4):
                p.emit("gpsimd", lambda g, sft=sft: g.memset(
                    masks[:, sft * 512:(sft + 1) * 512], 0.0), incs=[(GP, 1)])
                p.emit("gpsimd", lambda g, sft=sft: g.affine_select(
                    out=masks[:, sft * 512:(sft + 1) * 512],
                    in_=masks[:, sft * 512:(sft + 1) * 512],
                    compare_op=ALU.is_ge, fill=NEG,
                    base=-128 * sft, pattern=[[1, 512]],
                    channel_multiplier=-1),
                    waits=[(GP, GP.n)], incs=[(GP, 1)])
            return GP.n

        bank_free = {}
        xfree = [None, None]
        sqfree = [None]
        outfree = [None] * 2
        xrfree = [None] * 2

        def ln_stats(src, c0, c1, extra_first_wait=None):
            for hc in range(HK):
                sl = hc % 2
                p.emit("sync", dma(xb_sb[:, sl * 512:(sl + 1) * 512],
                                   src[:, hc, c0:c1]),
                       waits=[xfree[sl]] +
                             ([extra_first_wait] if hc == 0 else []),
                       incs=[(xb_ld[sl], 16)])
                t_x = xb_ld[sl].tok()
                p.emit("tensor", lambda t, sl=sl, hc=hc: t.matmul(
                    B[0][:], ones_sq[:],
                    xb_sb[:, sl * 512:(sl + 1) * 512],
                    start=(hc == 0), stop=(hc == HK - 1)),
                    waits=[t_x, (VE, ve_const)] +
                          ([bank_free.get(0)] if hc == 0 else []))
                p.emit("vector", lambda v, sl=sl: v.tensor_mul(
                    sq_sb[:],
                    xb_sb[:, sl * 512:(sl + 1) * 512],
                    xb_sb[:, sl * 512:(sl + 1) * 512]),
                    waits=[t_x, sqfree[0]], incs=[(VE, 1)])
                ve_sq = VE.n
                p.emit("tensor", lambda t, hc=hc: t.matmul(
                    B[1][:], ones_sq[:],
                    sq_sb[:],
                    start=(hc == 0), stop=(hc == HK - 1)),
                    waits=[(VE, ve_sq)] +
                          ([bank_free.get(1)] if hc == 0 else []),
                    incs=[(MM, 1)])
                xfree[sl] = (MM, MM.n)
                sqfree[0] = (MM, MM.n)
            return MM.n

        def ln_rows(mm_stats):
            mrow = rows[:, 0:512]
            srow = rows[:, 512:1024]
            m2row = rows[:, 1024:1536]
            vrow = rows[:, 1536:2048]
            rsrow = rows[:, 2048:2560]
            mrsrow = rows[:, 2560:3072]
            p.emit("vector", lambda v: v.tensor_scalar(
                mrow, B[0][0:1, :], INV_D, None, ALU.mult),
                waits=[(MM, mm_stats)])
            p.emit("vector", lambda v: v.tensor_scalar(
                srow, B[1][0:1, :], INV_D, None, ALU.mult))
            p.emit("vector", lambda v: v.tensor_mul(m2row, mrow, mrow))
            p.emit("vector", lambda v: v.tensor_sub(vrow, srow, m2row))
            p.emit("vector", lambda v: v.tensor_single_scalar(
                vrow, vrow, EPS, ALU.add), incs=[(VE, 1)])
            p.emit("scalar", lambda s: s.activation(
                srow, vrow, AF.Sqrt), waits=[(VE, VE.n)], incs=[(ACT, 1)])
            p.emit("vector", lambda v: v.reciprocal(rsrow, srow),
                   waits=[(ACT, ACT.n)])
            p.emit("vector", lambda v: v.tensor_mul(mrsrow, mrow, rsrow),
                   incs=[(VE, 1)])
            ve_rows = VE.n
            p.emit("tensor", lambda t: t.matmul(
                B[2][:], ones_col[:], rsrow, start=True, stop=True),
                waits=[(VE, ve_rows), bank_free.get(2)])
            p.emit("tensor", lambda t: t.matmul(
                B[3][:], ones_col[:], mrsrow, start=True, stop=True),
                waits=[bank_free.get(3)], incs=[(MM, 1)])
            return MM.n

        def ffn_pass(src, dst, first_wait=None):
            hdn_ready = [None] * IK
            wfree = [None, None]
            dfree = [None, None]
            gb_free = [None, None]
            sg_free = [None]
            obfree = [None] * 4
            hspfree = [None] * 4
            xnp_free = [None]
            xh = xnp_sb[:, 0:8192]
            xm = xnp_sb[:, 8192:16384]
            xl = xnp_sb[:, 16384:24576]
            for q5 in range(2):
                c0, c1 = q5 * 512, (q5 + 1) * 512
                mm_stats = ln_stats(src, c0, c1,
                                    first_wait if q5 == 0 else None)
                mm_bcast = ln_rows(mm_stats)
                for hc in range(HK):
                    sl = hc % 2
                    s5 = slice(hc * 512, (hc + 1) * 512)
                    p.emit("sync", dma(xb_sb[:, sl * 512:(sl + 1) * 512],
                                       src[:, hc, c0:c1]),
                           waits=[xfree[sl]], incs=[(xb_ld[sl], 16)])
                    t_x = xb_ld[sl].tok()
                    p.emit("vector", lambda v, sl=sl: v.tensor_mul(
                        tmp_sb[:], xb_sb[:, sl * 512:(sl + 1) * 512], B[2][:]),
                        waits=[t_x, (MM, mm_bcast)] +
                              ([xnp_free[0]] if hc == 0 else []))
                    p.emit("vector", lambda v: v.tensor_sub(
                        xnf_sb[:], tmp_sb[:], B[3][:]))
                    p.emit("vector", lambda v, s5=s5: v.tensor_copy(
                        xh[:, s5], xnf_sb[:]))
                    p.emit("vector", lambda v, s5=s5: v.tensor_sub(
                        xm[:, s5], xnf_sb[:], xh[:, s5]))
                    p.emit("vector", lambda v, s5=s5: v.tensor_sub(
                        r1f_sb[:], xnf_sb[:], xh[:, s5]))
                    p.emit("vector", lambda v, s5=s5: v.tensor_sub(
                        xl[:, s5], r1f_sb[:], xm[:, s5]), incs=[(VE, 1)])
                    xfree[sl] = (VE, VE.n)
                ve_xn = VE.n
                bank_free[2] = (VE, ve_xn)
                bank_free[3] = (VE, ve_xn)

                for isl in range(IK):
                    sl = isl % 2
                    gslab = gbf_sb[:, (2 * sl) * 2048:(2 * sl + 1) * 2048]
                    uslab = gbf_sb[:, (2 * sl + 1) * 2048:(2 * sl + 2) * 2048]
                    p.emit("gpsimd", dma(
                        gslab[:, 0:1024].rearrange("p (c n) -> p c n", n=128),
                        gw[:, isl, 0:8]), waits=[wfree[sl]],
                        incs=[(wgu_ld[sl], 16)])
                    p.emit("gpsimd", dma(
                        gslab[:, 1024:2048].rearrange("p (c n) -> p c n",
                                                      n=128),
                        gw[:, isl, 8:16]), incs=[(wgu_ld[sl], 16)])
                    p.emit("gpsimd", dma(
                        uslab[:, 0:1024].rearrange("p (c n) -> p c n", n=128),
                        uw[:, isl, 0:8]), incs=[(wgu_ld[sl], 16)])
                    p.emit("gpsimd", dma(
                        uslab[:, 1024:2048].rearrange("p (c n) -> p c n",
                                                      n=128),
                        uw[:, isl, 8:16]), incs=[(wgu_ld[sl], 16)])
                    t_w = wgu_ld[sl].tok()
                    gbank, ubank = B[2 * sl], B[2 * sl + 1]
                    for wslab, bank, gfree in ((gslab, gbank, gb_free[sl]),
                                               (uslab, ubank, None)):
                        first = True
                        for c in range(HK):
                            for xp in (xh, xm, xl):
                                p.emit("tensor", lambda t, wslab=wslab, c=c,
                                       bank=bank, xp=xp: t.matmul(
                                           bank[:],
                                           wslab[:, c * 128:(c + 1) * 128],
                                           xp[:, c * 512:(c + 1) * 512],
                                           start=(c == 0 and xp is xh),
                                           stop=(c == HK - 1 and xp is xl)),
                                       waits=([t_w, (VE, ve_xn), gfree]
                                              if first else []))
                                first = False
                        p.inc_last("tensor", MM, 1)
                        if wslab is gslab:
                            mm_gate = MM.n
                        else:
                            mm_up = MM.n
                    wfree[sl] = (MM, MM.n)
                    p.emit("scalar", lambda s, gbank=gbank, isl=isl:
                           s.activation(sg_sb[:], gbank[:], AF.Silu,
                                        scale=gs_sb[:, isl:isl + 1]),
                           waits=[(MM, mm_gate), sg_free[0], t_scales],
                           incs=[(ACT, 1)])
                    ac_silu = ACT.n
                    p.emit("vector", lambda v, ubank=ubank, isl=isl:
                           v.scalar_tensor_tensor(
                               hdn_sb[:, isl * 512:(isl + 1) * 512], ubank[:],
                               us_sb[:, isl:isl + 1], sg_sb[:],
                               ALU.mult, ALU.mult),
                           waits=[(MM, mm_up), (ACT, ac_silu)], incs=[(VE, 1)])
                    sg_free[0] = (VE, VE.n)
                    gb_free[sl] = (VE, VE.n)
                    hdn_ready[isl] = (VE, VE.n)
                xnp_free[0] = (MM, MM.n)

                for hg in range(4):
                    t_xr = [None, None]
                    for hc4 in range(2):
                        p.emit("sync", dma(
                            xr_sb[:, hc4 * 512:(hc4 + 1) * 512],
                            src[:, hg * 4 + hc4, c0:c1]),
                            waits=[xrfree[hc4]], incs=[(xr_ld, 16)])
                        xrfree[hc4] = None
                        t_xr[hc4] = xr_ld.tok()
                    for ic in range(IK):
                        sl = ic % 2
                        dslab = dbf_sb[:, sl * 512:(sl + 1) * 512]
                        p.emit("gpsimd", dma(dslab[:, 0:256],
                                             dw[:, ic, hg, 0:256]),
                               waits=[dfree[sl]], incs=[(wd_ld[sl], 16)])
                        p.emit("gpsimd", dma(dslab[:, 256:512],
                                             dw[:, ic, hg, 256:512]),
                               incs=[(wd_ld[sl], 16)])
                        t_d = wd_ld[sl].tok()
                        hsl = ic % 4
                        hs = hsp_sb[:, hsl * 1536:(hsl + 1) * 1536]
                        hh, hm, hl = hs[:, 0:512], hs[:, 512:1024], \
                            hs[:, 1024:1536]
                        hdn_ic = hdn_sb[:, ic * 512:(ic + 1) * 512]
                        p.emit("vector", lambda v, hh=hh, hdn_ic=hdn_ic:
                               v.tensor_copy(hh, hdn_ic),
                               waits=[hdn_ready[ic], hspfree[hsl]])
                        p.emit("vector", lambda v, hm=hm, hdn_ic=hdn_ic,
                               hh=hh: v.tensor_sub(hm, hdn_ic, hh))
                        p.emit("vector", lambda v, hdn_ic=hdn_ic, hh=hh:
                               v.tensor_sub(r1f_sb[:], hdn_ic, hh))
                        p.emit("vector", lambda v, hl=hl, hm=hm:
                               v.tensor_sub(hl, r1f_sb[:], hm),
                               incs=[(VE, 1)])
                        ve_hsp = VE.n
                        first = True
                        for hc4 in range(4):
                            for pi_, hp in enumerate((hh, hm, hl)):
                                st_ = (ic == 0 and pi_ == 0)
                                sp_ = (ic == IK - 1 and pi_ == 2)
                                p.emit("tensor", lambda t, dslab=dslab,
                                       hc4=hc4, hp=hp, st_=st_, sp_=sp_:
                                       t.matmul(
                                           B[4 + hc4][:],
                                           dslab[:, hc4 * 128:(hc4 + 1) * 128],
                                           hp, start=st_, stop=sp_),
                                       waits=([t_d, (VE, ve_hsp)] +
                                              ([obfree[0], obfree[1],
                                                obfree[2], obfree[3]]
                                               if ic == 0 else [])
                                              if first else []))
                                first = False
                        p.inc_last("tensor", MM, 1)
                        dfree[sl] = (MM, MM.n)
                        hspfree[hsl] = (MM, MM.n)
                    mm_down = MM.n
                    for hc4 in range(4):
                        hcg = hg * 4 + hc4
                        sl2 = hc4 % 2
                        if hc4 >= 2:
                            p.emit("sync", dma(
                                xr_sb[:, sl2 * 512:(sl2 + 1) * 512],
                                src[:, hcg, c0:c1]),
                                waits=[xrfree[sl2]], incs=[(xr_ld, 16)])
                            t_xr[sl2] = xr_ld.tok()
                        p.emit("vector", lambda v, sl2=sl2, hc4=hc4, hcg=hcg:
                               v.scalar_tensor_tensor(
                                   out_sb[:, sl2 * 512:(sl2 + 1) * 512],
                                   B[4 + hc4][:], dsh_sb[:, hcg:hcg + 1],
                                   xr_sb[:, sl2 * 512:(sl2 + 1) * 512],
                                   ALU.mult, ALU.add),
                               waits=[(MM, mm_down), t_xr[sl2],
                                      outfree[sl2]],
                               incs=[(VE, 1)])
                        obfree[hc4] = (VE, VE.n)
                        xrfree[sl2] = (VE, VE.n)
                        p.emit("sync", dma(
                            dst[:, hcg, c0:c1],
                            out_sb[:, sl2 * 512:(sl2 + 1) * 512]),
                            waits=[(VE, VE.n)], incs=[(out_st[sl2], 16)])
                        outfree[sl2] = out_st[sl2].tok()

        ffn_pass(xT, x_dram)
        barrier_p1 = all_store_toks()

        if phases >= 2:
            onfree = [None, None]
            for q5 in range(2):
                c0, c1 = q5 * 512, (q5 + 1) * 512
                mm_stats = ln_stats(x_dram, c0, c1,
                                    barrier_p1[0] if q5 == 0 else None)
                mm_bcast = ln_rows(mm_stats)
                for hc in range(HK):
                    sl = hc % 2
                    p.emit("sync", dma(xb_sb[:, sl * 512:(sl + 1) * 512],
                                       x_dram[:, hc, c0:c1]),
                           waits=[xfree[sl]], incs=[(xb_ld[sl], 16)])
                    t_x = xb_ld[sl].tok()
                    p.emit("vector", lambda v, sl=sl: v.tensor_mul(
                        tmp_sb[:], xb_sb[:, sl * 512:(sl + 1) * 512], B[2][:]),
                        waits=[t_x, (MM, mm_bcast), onfree[sl]])
                    p.emit("vector", lambda v, sl=sl: v.tensor_sub(
                        out_sb[:, sl * 512:(sl + 1) * 512], tmp_sb[:], B[3][:]),
                        incs=[(VE, 1)])
                    xfree[sl] = (VE, VE.n)
                    p.emit("sync", dma(
                        cc_xn_in[q5][:, hc * 512:(hc + 1) * 512],
                        out_sb[:, sl * 512:(sl + 1) * 512]),
                        waits=[(VE, VE.n)], incs=[(out_st[sl], 16)])
                    onfree[sl] = out_st[sl].tok()
                    outfree[sl] = out_st[sl].tok()
                bank_free[2] = (VE, VE.n)
                bank_free[3] = (VE, VE.n)
                p.emit("gpsimd", lambda g, q5=q5: g.collective_compute(
                    "AllGather", ALU.bypass,
                    replica_groups=[list(range(R))],
                    ins=[cc_xn_in[q5][:]], outs=[cc_xn_out[q5][:]]),
                    waits=[out_st[0].tok(), out_st[1].tok()], incs=[(CC, 1)])
            cc_xn_done = CC.n


        if phases >= 3:
            p.emit("sync", dma(wq_sb.rearrange("p (c n) -> p c n", n=256), wq),
                   waits=[(CC, cc_xn_done - 1)], incs=[(wqkv_ld, 16)])
            p.emit("sync", dma(wk_sb.rearrange("p (c n) -> p c n", n=256), wk),
                   incs=[(wqkv_ld, 16)])
            p.emit("sync", dma(wv_sb.rearrange("p (c n) -> p c n", n=256), wv),
                   incs=[(wqkv_ld, 16)])
            t_wqkv = wqkv_ld.tok()
            gp_masks = build_masks()
            xn_view = [cco.rearrange("(r p) (c t) -> r p c t", p=128, t=512)
                       for cco in cc_xn_out]
            xnt_free = [None, None]
            ev_free = [None] * 8
            b7_free = None
            for ni, nt in enumerate(list(range(0, 16, 2)) +
                                    list(range(1, 16, 2))):
                blk = nt // 2
                q5 = nt % 2
                sl = ni % 2
                xnt = xnt_sb[:, sl * HK * 512:(sl + 1) * HK * 512]
                p.emit("sync", dma(
                    xnt.rearrange("p (c t) -> p c t", t=512),
                    xn_view[q5][blk, :, :, :]),
                    waits=[xnt_free[sl], (CC, cc_xn_done - 1 + q5)],
                    incs=[(xnt_ld[sl], 16)])
                t_xnt = xnt_ld[sl].tok()
                for pi, wsb in ((0, wq_sb), (1, wk_sb), (2, wv_sb)):
                    for dc in range(2):
                        bank = B[pi * 2 + dc]
                        for c in range(HK):
                            p.emit("tensor", lambda t, wsb=wsb, c=c, dc=dc,
                                   bank=bank, xnt=xnt: t.matmul(
                                       bank[:],
                                       wsb[:, c * 256 + dc * 128:
                                           c * 256 + dc * 128 + 128],
                                       xnt[:, c * 512:(c + 1) * 512],
                                       start=(c == 0), stop=(c == HK - 1)),
                                   waits=([t_xnt, t_wqkv,
                                           bank_free.get(pi * 2 + dc)]
                                          if c == 0 else []))
                        p.inc_last("tensor", MM, 1)
                        bank_free[pi * 2 + dc] = None
                mm_qkv = MM.n
                xnt_free[sl] = (MM, MM.n)
                for pi, dest in ((0, qd), (1, kd)):
                    for dc in range(2):
                        bank = B[pi * 2 + dc]
                        mm_bank = mm_qkv - 5 + (pi * 2 + dc)
                        esl = pi * 2 + dc
                        ev = ev_sb[:, esl * 512:(esl + 1) * 512]
                        p.emit("vector", lambda v, ev=ev, bank=bank: v.tensor_copy(
                            ev, bank[:]),
                            waits=[(MM, mm_bank), ev_free[esl]], incs=[(VE, 1)])
                        bank_free[pi * 2 + dc] = (VE, VE.n)
                        p.emit("sync", dma(dest[dc, :, nt * 512:nt * 512 + 512],
                                           ev),
                               waits=[(VE, VE.n)], incs=[(ev_st[esl], 16)])
                        st_ev = ev_st[esl].tok()
                        p.emit("vector", lambda v, ev=ev: v.tensor_mul(
                            tmp_sb[:], ev, ev), waits=[b7_free], incs=[(VE, 1)])
                        p.emit("tensor", lambda t: t.matmul(
                            B[7][:], ones_sq[:], tmp_sb[:],
                            start=True, stop=True),
                            waits=[(VE, VE.n)], incs=[(MM, 1)])
                        p.emit("vector", lambda v: v.reduce_max(
                            rows[:, 0:1], B[7][0:1, :], axis=AX.X),
                            waits=[(MM, MM.n)], incs=[(VE, 1)])
                        b7_free = (VE, VE.n)
                        t_red = (VE, VE.n)
                        if pi == 0:
                            p.emit("vector", lambda v, dc=dc, nt=nt: v.tensor_copy(
                                maxQn[0:1, dc * 16 + nt:dc * 16 + nt + 1],
                                rows[:, 0:1]), waits=[t_red], incs=[(VE, 1)])
                        else:
                            p.emit("vector", lambda v, dc=dc: v.tensor_tensor(
                                maxKn[0:1, dc:dc + 1], maxKn[0:1, dc:dc + 1],
                                rows[:, 0:1], ALU.max), waits=[t_red],
                                incs=[(VE, 1)])
                        ev_free[esl] = st_ev
                for dc in range(2):
                    bank = B[4 + dc]
                    mm_bank = mm_qkv - 1 + dc
                    esl = 4 + dc
                    ev = ev_sb[:, esl * 512:(esl + 1) * 512]
                    p.emit("vector", lambda v, ev=ev, bank=bank: v.tensor_copy(
                        ev, bank[:]), waits=[(MM, mm_bank), ev_free[esl]],
                        incs=[(VE, 1)])
                    ve_ev = VE.n
                    bank_free[4 + dc] = (VE, VE.n)
                    for tt in range(4):
                        p.emit("tensor", lambda t, ev=ev, tt=tt: t.transpose(
                            B[6][:, tt * 128:(tt + 1) * 128],
                            ev[:, tt * 128:(tt + 1) * 128], ident[:]),
                            waits=([(VE, ve_ev), (GP, gp_const),
                                    bank_free.get(6)] if tt == 0 else []))
                    p.inc_last("tensor", MM, 1)
                    ev_free[esl] = (MM, MM.n)
                    esl2 = 6 + dc
                    ev2 = ev_sb[:, esl2 * 512:(esl2 + 1) * 512]
                    p.emit("vector", lambda v, ev2=ev2: v.tensor_copy(
                        ev2, B[6][:]), waits=[(MM, MM.n), ev_free[esl2]],
                        incs=[(VE, 1)])
                    bank_free[6] = (VE, VE.n)
                    p.emit("sync", dma(
                        vtm[nt * 512:(nt + 1) * 512,
                            dc * 128:(dc + 1) * 128].rearrange(
                                "(tt p) d -> p tt d", p=128),
                        ev2.rearrange("p (tt d) -> p tt d", d=128)),
                        waits=[(VE, VE.n)], incs=[(vt_st[dc], 16)])
                    ev_free[esl2] = vt_st[dc].tok()
            barrier_qkv = [s.tok() for s in ev_st + vt_st]

            for dc in range(NHL):
                p.emit("vector", lambda v, dc=dc: v.tensor_scalar(
                    negCs[0:1, dc * 16:(dc + 1) * 16],
                    maxQn[0:1, dc * 16:(dc + 1) * 16],
                    maxKn[0:1, dc:dc + 1], None, ALU.mult), incs=[(VE, 1)])
            p.emit("scalar", lambda s: s.activation(negCs[:], negCs[:], AF.Sqrt),
                   waits=[(VE, VE.n)], incs=[(ACT, 1)])
            p.emit("vector", lambda v: v.tensor_scalar(
                negCs[:], negCs[:], -SCALE, None, ALU.mult),
                waits=[(ACT, ACT.n)], incs=[(VE, 1)])
            ve_ctab = VE.n


        if phases >= 4:
            p.emit("sync", dma(wo_sb.rearrange("p (c n) -> p c n", n=D), wo),
                   waits=barrier_qkv, incs=[(wo_ld, 16)])
            t_wo = wo_ld.tok()
            kfree = [None] * 4
            vfree = [None] * 4
            pfree = [None] * 4
            qfree = [None, None]
            l_free = None
            o_free = None
            negc_free = None
            linv_free = None
            oT_free = [None, None]
            wo_bfree = [None, None]
            wo_outfree = [outfree[0], outfree[1]]
            sbank_free = [None, None]
            for qc in range(16):
                for h in range(NHL):
                    qsl = (qc * NHL + h) % 2
                    qv = q_sb[:, qsl * 512:(qsl + 1) * 512]
                    p.emit("sync", dma(qv, qd[h, :, qc * 512:qc * 512 + 512]),
                           waits=[qfree[qsl]] +
                                 (barrier_qkv if qc == 0 and h == 0 else []),
                           incs=[(q_ld, 16)])
                    t_q = q_ld.tok()
                    p.emit("tensor", lambda t, h=h, qc=qc: t.matmul(
                        B[4][:, 0:1], ones_col[:],
                        negCs[0:1, h * 16 + qc:h * 16 + qc + 1],
                        start=True, stop=True),
                        waits=[(VE, ve_ctab), negc_free], incs=[(MM, 1)])
                    p.emit("vector", lambda v: v.tensor_copy(
                        negc_sb[:], B[4][:, 0:1]),
                        waits=[(MM, MM.n)], incs=[(VE, 1)])
                    ve_negc = VE.n
                    for j in range(qc + 1):
                        ksl = j % 4
                        vsl = j % 4
                        p.emit("sync", dma(k_sb[:, ksl * 512:(ksl + 1) * 512],
                                           kd[h, :, j * 512:j * 512 + 512]),
                               waits=[kfree[ksl]], incs=[(k_ld[ksl], 16)])
                        t_k = k_ld[ksl].tok()
                        p.emit("sync", dma(
                            v_sb[:, vsl * 512:(vsl + 1) * 512].rearrange(
                                "p (tt d) -> p tt d", d=128),
                            vtm[j * 512:(j + 1) * 512,
                                h * 128:(h + 1) * 128].rearrange(
                                    "(tt p) d -> p tt d", p=128)),
                            waits=[vfree[vsl]], incs=[(v_ld[vsl], 16)])
                        t_v = v_ld[vsl].tok()
                        diag = (j == qc)
                        first = (j == 0)
                        last = (j == qc)
                        ac_exps = [None] * 4
                        for sft in range(4):
                            sb = sft % 2
                            p.emit("tensor", lambda t, sb=sb, ksl=ksl, sft=sft,
                                   qv=qv: t.matmul(
                                       B[sb][:],
                                       k_sb[:, ksl * 512 + sft * 128:
                                            ksl * 512 + sft * 128 + 128],
                                       qv, start=True, stop=True),
                                   waits=[t_k, t_q, sbank_free[sb]],
                                   incs=[(MM, 1)])
                            mm_s = MM.n
                            if diag:
                                p.emit("vector", lambda v, sb=sb, sft=sft:
                                       v.tensor_add(B[sb][:], B[sb][:],
                                                    masks[:, sft * 512:
                                                          (sft + 1) * 512]),
                                       waits=[(MM, mm_s), (GP, gp_masks)],
                                       incs=[(VE, 1)])
                                dep = (VE, VE.n)
                            else:
                                dep = (MM, mm_s)
                            p.emit("scalar", lambda s, sb=sb, sft=sft:
                                   s.activation(
                                       p_sb[:, sft * 512:(sft + 1) * 512],
                                       B[sb][:], AF.Exp, bias=negc_sb[:],
                                       scale=SCALE),
                                   waits=[dep, pfree[sft], (VE, ve_negc)],
                                   incs=[(ACT, 1)])
                            ac_exps[sft] = (ACT, ACT.n)
                            sbank_free[sb] = (ACT, ACT.n)
                        for sft in range(4):
                            p.emit("tensor", lambda t, sft=sft, first=first,
                                   last=last: t.matmul(
                                       B[2][:], ones_sq[:],
                                       p_sb[:, sft * 512:(sft + 1) * 512],
                                       start=(first and sft == 0),
                                       stop=(last and sft == 3)),
                                   waits=[ac_exps[sft]] +
                                         ([l_free] if first and sft == 0 else []))
                            p.emit("tensor", lambda t, sft=sft, vsl=vsl,
                                   first=first, last=last: t.matmul(
                                       B[3][:], v_sb[:, vsl * 512 + sft * 128:
                                                     vsl * 512 + sft * 128 + 128],
                                       p_sb[:, sft * 512:(sft + 1) * 512],
                                       start=(first and sft == 0),
                                       stop=(last and sft == 3)),
                                   waits=[t_v] +
                                         ([o_free] if first and sft == 0 else []),
                                   incs=[(MM, 1)])
                            pfree[sft] = (MM, MM.n)
                        kfree[ksl] = (MM, MM.n)
                        vfree[vsl] = (MM, MM.n)
                    qfree[qsl] = (MM, MM.n)
                    mm_attn = MM.n
                    negc_free = (ACT, ACT.n)
                    p.emit("vector", lambda v: v.reciprocal(
                        linv_sb[:], B[2][:]),
                        waits=[(MM, mm_attn)], incs=[(VE, 1)])
                    l_free = (VE, VE.n)
                    p.emit("vector", lambda v, h=h: v.tensor_mul(
                        oT_sb[:, h * 512:(h + 1) * 512], B[3][:], linv_sb[:]),
                        waits=[oT_free[h]], incs=[(VE, 1)])
                    o_free = (VE, VE.n)
                ve_oT = VE.n
                for tt in range(4):
                    for db in range(4):
                        wsl = (tt * 4 + db) % 2
                        bank = B[6 + wsl]
                        for dc in range(2):
                            p.emit("tensor", lambda t, tt=tt, db=db, dc=dc,
                                   bank=bank: t.matmul(
                                       bank[:],
                                       oT_sb[:, dc * 512 + tt * 128:
                                             dc * 512 + tt * 128 + 128],
                                       wo_sb[:, dc * D + db * 512:
                                             dc * D + db * 512 + 512],
                                       start=(dc == 0), stop=(dc == 1)),
                                   waits=([(VE, ve_oT), t_wo, wo_bfree[wsl]]
                                          if dc == 0 else []))
                        p.inc_last("tensor", MM, 1)
                        mm_wo = MM.n
                        p.emit("vector", lambda v, wsl=wsl, bank=bank:
                               v.tensor_copy(out_sb[:, wsl * 512:(wsl + 1) * 512],
                                             bank[:]),
                               waits=[(MM, mm_wo), wo_outfree[wsl]],
                               incs=[(VE, 1)])
                        wo_bfree[wsl] = (VE, VE.n)
                        p.emit("sync", dma(
                            attn_part[qc * 512 + tt * 128:
                                      qc * 512 + tt * 128 + 128,
                                      db * 512:(db + 1) * 512],
                            out_sb[:, wsl * 512:(wsl + 1) * 512]),
                            waits=[(VE, VE.n)], incs=[(out_st[wsl], 16)])
                        wo_outfree[wsl] = out_st[wsl].tok()
                oT_free[0] = (MM, MM.n)
                oT_free[1] = (MM, MM.n)
            outfree[0] = wo_outfree[0]
            outfree[1] = wo_outfree[1]


        if phases >= 5:
            p.emit("gpsimd", lambda g: g.collective_compute(
                "ReduceScatter", ALU.add, replica_groups=[list(range(R))],
                ins=[attn_part[:]], outs=[attn_own[:]]),
                waits=all_store_toks(), incs=[(CC, 1)])
            cc_rs_done = CC.n

            at_free = None
            xs_free = None
            x2_free = outfree[0]
            for t8 in range(8):
                for hg in range(4):
                    p.emit("sync", dma(
                        xb_sb[:, 0:512],
                        attn_own[t8 * 128:(t8 + 1) * 128,
                                 hg * 512:(hg + 1) * 512]),
                        waits=[(CC, cc_rs_done), at_free, xfree[0]],
                        incs=[(xb_ld[0], 16)])
                    xfree[0] = None
                    t_at = xb_ld[0].tok()
                    p.emit("sync", dma(
                        xr_sb[:, 0:512].rearrange("p (hc t) -> p hc t", t=128),
                        x_dram[:, hg * 4:(hg + 1) * 4,
                               t8 * 128:(t8 + 1) * 128]),
                        waits=[xs_free, xrfree[0]], incs=[(xr_ld, 16)])
                    xrfree[0] = None
                    t_xs = xr_ld.tok()
                    for hc4 in range(4):
                        p.emit("tensor", lambda t, hc4=hc4: t.transpose(
                            B[0][:, hc4 * 128:(hc4 + 1) * 128],
                            xb_sb[:, hc4 * 128:(hc4 + 1) * 128], ident[:]),
                            waits=([t_at, bank_free.get(0)]
                                   if hc4 == 0 else []))
                    p.inc_last("tensor", MM, 1)
                    at_free = (MM, MM.n)
                    p.emit("vector", lambda v: v.tensor_add(
                        out_sb[:, 0:512], B[0][:], xr_sb[:, 0:512]),
                        waits=[(MM, MM.n), t_xs, x2_free], incs=[(VE, 1)])
                    bank_free[0] = (VE, VE.n)
                    xs_free = (VE, VE.n)
                    p.emit("sync", dma(
                        x_dram[:, hg * 4:(hg + 1) * 4,
                               t8 * 128:(t8 + 1) * 128],
                        out_sb[:, 0:512].rearrange("p (hc t) -> p hc t", t=128)),
                        waits=[(VE, VE.n)], incs=[(out_st[0], 16)])
                    x2_free = out_st[0].tok()
            outfree[0] = x2_free


        if phases >= 6:
            bank_free.clear()
            ffn_pass(x_dram, yT, first_wait=x2_free)


        if probe == "tabs":
            p.emit("sync", dma(yT[0:1, 0, 0:32], negCs[:]),
                   waits=all_store_toks() + [(CC, CC.n), (MM, MM.n),
                                             (VE, VE.n), (ACT, ACT.n)],
                   incs=[(out_st[3], 16)])
            p.emit("sync", dma(yT[0:1, 0, 32:64], maxQn[:]),
                   incs=[(out_st[3], 16)])
            p.emit("sync", dma(yT[0:1, 0, 64:66], maxKn[:]),
                   incs=[(out_st[3], 16)])
        elif probe is not None:
            yflat = yT.rearrange("p c t -> p (c t)")
            srcs = {
                "x_dram": x_dram.rearrange("p c t -> p (c t)"),
                "cc_xn_in": cc_xn_in[0][:],
                "qd": qd.rearrange("h p s -> p h s"),
                "kd": kd.rearrange("h p s -> p h s"),
                "vtm": vtm.rearrange("(a p) d -> p a d", p=128),
                "part0": attn_part[0:1024].rearrange("(a p) d -> p a d", p=128),
                "attn_own": attn_own.rearrange("(a p) d -> p a d", p=128),
            }
            sp = srcs[probe]
            yv = yflat if len(sp.shape) == 2 else yflat.rearrange(
                "p (a d) -> p a d", d=sp.shape[-1])
            p.emit("sync", dma(yv, sp),
                   waits=all_store_toks() + [(CC, CC.n), (MM, MM.n),
                                             (VE, VE.n), (ACT, ACT.n)],
                   incs=[(out_st[3], 16)])

        p.emit("sync", None,
               waits=all_store_toks() + [(MM, MM.n), (VE, VE.n),
                                         (ACT, ACT.n), (CC, CC.n),
                                         (GP, GP.n)])

        p.finalize(block)

    return nc


def kernel(L0_attn_x, L0_attn_Wq, L0_attn_Wk, L0_attn_Wv, L0_attn_Wo,
           L0_ffn_gate_w, L0_ffn_up_w, L0_ffn_down_w,
           L0_ffn_gate_s, L0_ffn_up_s, L0_ffn_down_s):
    x = np.ascontiguousarray(np.asarray(L0_attn_x, np.float32)[0])
    Wq = np.asarray(L0_attn_Wq, np.float32)
    Wk = np.asarray(L0_attn_Wk, np.float32)
    Wv = np.asarray(L0_attn_Wv, np.float32)
    Wo = np.asarray(L0_attn_Wo, np.float32)
    gwf = np.asarray(L0_ffn_gate_w, np.float32)
    uwf = np.asarray(L0_ffn_up_w, np.float32)
    dwf = np.asarray(L0_ffn_down_w, np.float32)
    gsf = np.asarray(L0_ffn_gate_s, np.float32)
    usf = np.asarray(L0_ffn_up_s, np.float32)
    dsf = np.asarray(L0_ffn_down_s, np.float32)

    g_host = np.ascontiguousarray(
        gwf.reshape(IK, 128, HK, 128).transpose(3, 0, 2, 1))
    u_host = np.ascontiguousarray(
        uwf.reshape(IK, 128, HK, 128).transpose(3, 0, 2, 1))
    d_host = np.ascontiguousarray(
        dwf.reshape(4, 512, IK, 128).transpose(3, 2, 0, 1))
    gs_host = np.ascontiguousarray(gsf.reshape(IK, 128).T)
    us_host = np.ascontiguousarray(usf.reshape(IK, 128).T)
    ds_host = np.ascontiguousarray(dsf.reshape(HK, 128).T)

    nc = build()
    in_maps = []
    for r in range(R):
        xs = x[TL * r:TL * (r + 1)].T
        x_host = np.ascontiguousarray(
            xs.reshape(HK, 128, TL).transpose(1, 0, 2))
        wq_r = np.ascontiguousarray(
            Wq[:, 256 * r:256 * (r + 1)].reshape(HK, 128, 256)
            .transpose(1, 0, 2))
        wk_r = np.ascontiguousarray(
            Wk[:, 256 * r:256 * (r + 1)].reshape(HK, 128, 256)
            .transpose(1, 0, 2))
        wv_r = np.ascontiguousarray(
            Wv[:, 256 * r:256 * (r + 1)].reshape(HK, 128, 256)
            .transpose(1, 0, 2))
        wo_r = np.ascontiguousarray(
            Wo[256 * r:256 * (r + 1), :].reshape(2, 128, D)
            .transpose(1, 0, 2))
        in_maps.append({
            "xT": x_host, "wq": wq_r, "wk": wk_r, "wv": wv_r, "wo": wo_r,
            "gw": g_host, "uw": u_host, "dw": d_host,
            "gs": gs_host, "us": us_host, "ds": ds_host,
        })

    res = run_bass_kernel_spmd(nc, in_maps, core_ids=list(range(R)))

    y = np.empty((1, S, D), np.float32)
    for r in range(R):
        yT = np.asarray(res.results[r]["yT"])
        y[0, TL * r:TL * (r + 1), :] = (
            yT.transpose(2, 1, 0).reshape(TL, D))
    return y
